# version 10
# baseline (speedup 1.0000x reference)
import numpy as np
import concourse.bass as bass
import concourse.mybir as mybir
from concourse.alu_op_type import AluOpType
from concourse.bass_utils import run_bass_kernel_spmd
from contextlib import ExitStack

F32 = mybir.dt.float32
BF16 = mybir.dt.bfloat16
I32 = mybir.dt.int32
AF = mybir.ActivationFunctionType
AX = mybir.AxisListType.X

D = 2048
S = 2048
IC = D // 128
HID = 8192
HC = HID // 128
TT = 512
NT = S // TT
NSEQ = 2
NH = 16
LAT = 512
LC = LAT // 128
EPS = 1e-5
ISQ = float(1.0 / np.sqrt(128.0))


def pack_w(w):
    o, i = w.shape
    oc, ic = o // 128, i // 128
    return np.ascontiguousarray(
        w.reshape(oc, 128, ic, 128).transpose(0, 3, 2, 1).reshape(o, i))


def pack_b(b):
    return np.ascontiguousarray(b.reshape(-1, 128).T)


class Sem:
    __slots__ = ("h", "name", "n")

    def __init__(self, h, name):
        self.h, self.name, self.n = h, name, 0


class Prog:
    ENG = ("tensor", "vector", "scalar", "gpsimd", "sync")

    def __init__(self, nc, stack):
        self.nc = nc
        self.stack = stack
        self.ops = {e: [] for e in self.ENG}
        self.seen = {e: {} for e in self.ENG}
        self._sems = []

    def sem(self, name):
        s = Sem(self.stack.enter_context(self.nc.semaphore(name)), name)
        self._sems.append(s)
        return s

    def emit(self, eng, fn, wait=(), inc=None, inc_k=1):
        waits = []
        seen = self.seen[eng]
        for s, v in wait:
            if v <= 0 or seen.get(s.name, 0) >= v:
                continue
            seen[s.name] = v
            waits.append((s, v))
        if inc is not None:
            inc.n += inc_k
        self.ops[eng].append((fn, waits, inc, inc_k))

    def flush(self, name=""):
        ops = self.ops
        self.ops = {e: [] for e in self.ENG}
        active = [e for e in self.ENG if ops[e]]
        if not active:
            return
        with self.nc.Block() as blk:
            for e in active:
                def body(eng, _ops=ops[e]):
                    for fn, waits, inc, inc_k in _ops:
                        for s, v in waits:
                            eng.wait_ge(s.h, v)
                        r = fn(eng)
                        if inc is not None:
                            r.then_inc(inc.h, inc_k)
                getattr(blk, e)(body)
        for s in self._sems:
            if not s.name.startswith("d_"):
                for e in self.ENG:
                    self.seen[e][s.name] = s.n


class MK:
    def __init__(self):
        self.nc = bass.Bass("TRN2", target_bir_lowering=False, debug=False)
        nc = self.nc
        self.st = ExitStack()

        t = {}
        t["x"] = nc.dram_tensor("x", [NSEQ * D, S], F32, kind="ExternalInput").ap()
        t["y"] = nc.dram_tensor("y", [NSEQ * D, S], F32, kind="ExternalOutput").ap()
        for n, shp in [
            ("wq", [D, D]), ("wk", [D, D]), ("wv", [D, D]), ("wo", [D, D]),
            ("wqd", [LAT, D]), ("wkvd", [LAT, D]),
            ("wqu", [D // 2, LAT]), ("wku", [D // 2, LAT]),
            ("wvu", [D, LAT]), ("wob", [D, D]),
            ("wg", [HID, D]), ("wu", [HID, D]), ("wd", [D, HID]),
        ]:
            t[n] = nc.dram_tensor(n, shp, F32, kind="ExternalInput").ap()
        for n, noc in [("bq", IC), ("bk", IC), ("bv", IC), ("bo", IC),
                       ("bqd", LC), ("bkvd", LC), ("bqu", 8), ("bku", 8),
                       ("bvu", IC), ("bob", IC)]:
            t[n] = nc.dram_tensor(n, [128, noc], F32, kind="ExternalInput").ap()
        t["xs0"] = nc.dram_tensor("xs0", [D, S], F32).ap()
        t["xs1"] = nc.dram_tensor("xs1", [D, S], F32).ap()
        t["gated"] = nc.dram_tensor("gated", [128, NT * HC * TT], BF16).ap()
        t["aout"] = nc.dram_tensor("aout", [128, NT * IC * TT], BF16).ap()
        t["qlat_d"] = nc.dram_tensor("qlat_d", [128, LC * S], BF16).ap()
        t["kvlat_d"] = nc.dram_tensor("kvlat_d", [128, LC * S], BF16).ap()
        t["wo_bf"] = nc.dram_tensor("wo_bf", [D, D], BF16).ap()
        t["wob_bf"] = nc.dram_tensor("wob_bf", [D, D], BF16).ap()
        t["wg_bf"] = nc.dram_tensor("wg_bf", [HID, D], BF16).ap()
        t["wu_bf"] = nc.dram_tensor("wu_bf", [HID, D], BF16).ap()
        t["wd_bf"] = nc.dram_tensor("wd_bf", [D, HID], BF16).ap()
        self.t = t

        sbp = lambda n, shp, dt: self.st.enter_context(nc.sbuf_tensor(n, shp, dt))
        self.ln_x = None
        self.ones = sbp("ones", [128, 128], BF16)
        self.ident = sbp("ident", [128, 128], BF16)
        self.bias = {n: sbp("b_" + n, [128, {"bq": IC, "bk": IC, "bv": IC,
                                             "bo": IC, "bqd": LC, "bkvd": LC,
                                             "bqu": 8, "bku": 8, "bvu": IC,
                                             "bob": IC}[n]], F32)
                     for n in ("bq", "bk", "bv", "bo", "bqd", "bkvd",
                               "bqu", "bku", "bvu", "bob")}

        self.negb = {n: sbp("nb_" + n, [128, IC], F32) for n in ("bq", "bk")}
        self.COS = sbp("COS", [128, 2 * S], BF16)
        self.SIN = sbp("SIN", [128, 2 * S], BF16)
        self.cons = sbp("cons", [128, 8], F32)
        self.CONSTS = [EPS, 1e-8, 1e-30]

        self.p = Prog(nc, self.st)
        self.s = {n: self.p.sem(n) for n in
                  ("d_x0", "d_x1", "d_w0", "d_w1", "d_g0", "d_g1",
                   "d_s0", "d_s1", "d_pm", "d_q0", "d_q1", "d_q2", "d_q3",
                   "pe", "act", "dve", "pool")}
        self.dq = [self.s[f"d_q{i}"] for i in range(4)]
        self.dx = [self.s["d_x0"], self.s["d_x1"]]
        self.dw = [self.s["d_w0"], self.s["d_w1"]]
        self.dg = [self.s["d_g0"], self.s["d_g1"]]
        self.ds = [self.s["d_s0"], self.s["d_s1"]]
        self.dpm = self.s["d_pm"]
        self.ffn_call = 0
        self.prepc = {}
        self._uid = 0

    def sb(self, st, name, shp, dt):
        self._uid += 1
        return st.enter_context(
            self.nc.sbuf_tensor(f"{name}_{self._uid}", shp, dt))

    def pt(self, st, name, shp, dt):
        self._uid += 1
        return st.enter_context(
            self.nc.psum_tensor(f"{name}_{self._uid}", shp, dt))

    def ld(self, queue, sem, out, in_, wait=()):
        self.p.emit(queue, lambda e, o=out, i=in_: e.dma_start(out=o, in_=i),
                    wait=wait, inc=sem, inc_k=16)
        return sem.n

    def stq(self, sem, out, in_, wait=()):
        self.p.emit("sync", lambda e, o=out, i=in_: e.dma_start(out=o, in_=i),
                    wait=wait, inc=sem, inc_k=16)
        return sem.n

    def flush(self, name=""):
        dsems = [v for k, v in self.s.items() if k.startswith("d_")]
        self.p.emit("sync", lambda e: e.nop(),
                    wait=[(d, d.n) for d in dsems])
        self.p.flush(name)

    def setup(self):
        p, s, t = self.p, self.s, self.t
        p.emit("gpsimd", lambda e: e.memset(self.ones[:, :], 1.0),
               inc=s["pool"])
        p.emit("gpsimd", lambda e: e.affine_select(
            self.ident[:, :], self.ones[:, :], [[1, 128]],
            AluOpType.is_equal, 0.0, base=0, channel_multiplier=-1),
            wait=[(s["pool"], s["pool"].n)])
        for i, v in enumerate(self.CONSTS):
            ap = self.cons[:, i:i + 1]
            p.emit("gpsimd", lambda e, ap=ap, v=v: e.memset(ap, v))
            self.nc.const_aps.register(F32, float(v), ap)
        for n, b in self.bias.items():
            noc = b.shape[1]
            self.ld("sync", self.dpm, b[:, 0:noc], t[n][:, :])
        for n in ("bq", "bk"):
            p.emit("vector", lambda e, n=n: e.tensor_scalar_mul(
                self.negb[n][:, :], self.bias[n][:, :], -1.0),
                wait=[(self.dpm, self.dpm.n)])
        self.flush("setup")

    def ln(self, src, row0=0):
        nc, p, s = self.nc, self.p, self.s
        st = ExitStack()
        xf = [self.sb(st, f"ln_xf{i}", [128, IC * TT], F32)
              for i in range(2)]
        xb = [self.sb(st, "ln_xb", [128, IC * TT], BF16)] * 2
        xq = [self.sb(st, "ln_xq", [128, IC * TT], BF16)] * 2
        mpl = self.sb(st, "ln_m", [128, TT], F32)
        vpl = self.sb(st, "ln_v", [128, TT], F32)
        tpl = self.sb(st, "ln_t", [128, TT], F32)
        ntd = self.sb(st, "ln_ntd", [128, 2 * TT], F32)
        ntp = self.sb(st, "ln_ntp", [128, 2 * TT], F32)
        ps = [self.pt(st, f"ln_ps{i}", [128, 2 * TT], F32)
              for i in range(2)]
        pe, act, dve, pool = (s[k] for k in ("pe", "act", "dve", "pool"))
        dx = self.dx

        def tile(tt):
            return src[row0:row0 + D, tt * TT:(tt + 1) * TT].rearrange(
                "(ic p) t -> p ic t", p=128)

        def xfv(i):
            return xf[i][:, :].rearrange("p (ic t) -> p ic t", t=TT)

        ldc, sum_done = {}, {}
        cast_done, sq_done, stats_done, fin_done = {}, {}, {}, {}
        npool_done, ndve_done = {}, {}

        def fin_norm(tt):
            b = tt % 2
            pst = ps[b]
            p.emit("vector", lambda e, pst=pst: e.tensor_scalar_mul(
                mpl[:, :], pst[:, 0:TT], 1.0 / D),
                wait=[(pe, stats_done[tt]), (pool, npool_done.get(tt - 1, 0)),
                      (dve, ndve_done.get(tt - 1, 0))])
            p.emit("vector", lambda e, pst=pst: e.tensor_scalar_mul(
                vpl[:, :], pst[:, TT:2 * TT], 1.0 / D), inc=dve)
            mv_done = dve.n
            p.emit("vector", lambda e: e.tensor_mul(tpl[:, :], mpl[:, :], mpl[:, :]),
                   wait=[(dve, mv_done)], inc=dve)
            sq_mm = dve.n
            p.emit("vector", lambda e: e.tensor_sub(vpl[:, :], vpl[:, :], tpl[:, :]),
                   wait=[(dve, sq_mm)], inc=dve)
            var_done = dve.n
            p.emit("scalar", lambda e: e.activation(
                tpl[:, :], vpl[:, :], AF.Ln, bias=EPS),
                wait=[(dve, var_done)], inc=act)
            ln_done = act.n
            p.emit("scalar", lambda e: e.activation(
                tpl[:, :], tpl[:, :], AF.Exp, scale=-0.5),
                wait=[(act, ln_done)], inc=act)
            rstd_done = act.n
            fin_done[tt] = dve.n
            evens = [ic for ic in range(IC) if ic % 2 == 0]
            odds = [ic for ic in range(IC) if ic % 2 == 1]
            NW = 2
            for w0 in range(0, 8, NW):
                for k in range(NW):
                    ic = evens[w0 + k]
                    p.emit("vector", lambda e, b=b, ic=ic, k=k: e.tensor_sub(
                        ntd[:, k * TT:(k + 1) * TT],
                        xf[b][:, ic * TT:(ic + 1) * TT], mpl[:, :]),
                        wait=[(act, rstd_done), (dve, dve.n)] if k == 0 else [],
                        inc=dve if k == NW - 1 else None)
                sub_d = dve.n
                for k in range(NW):
                    ic = evens[w0 + k]
                    dst = self.ln_x[:, ic * S + tt * TT: ic * S + (tt + 1) * TT]
                    p.emit("vector", lambda e, dst=dst, k=k: e.tensor_mul(
                        dst, ntd[:, k * TT:(k + 1) * TT], tpl[:, :]),
                        wait=[(dve, sub_d)] if k == 0 else [],
                        inc=dve if k == NW - 1 else None)
            for w0 in range(0, 8, NW):
                for k in range(NW):
                    ic = odds[w0 + k]
                    p.emit("gpsimd", lambda e, b=b, ic=ic, k=k: e.tensor_sub(
                        ntp[:, k * TT:(k + 1) * TT],
                        xf[b][:, ic * TT:(ic + 1) * TT], mpl[:, :]),
                        wait=([(act, rstd_done), (pool, pool.n)]
                              if k == 0 else []),
                        inc=pool if k == NW - 1 else None)
                sub_p = pool.n
                for k in range(NW):
                    ic = odds[w0 + k]
                    dst = self.ln_x[:, ic * S + tt * TT: ic * S + (tt + 1) * TT]
                    p.emit("gpsimd", lambda e, dst=dst, k=k: e.tensor_mul(
                        dst, ntp[:, k * TT:(k + 1) * TT], tpl[:, :]),
                        wait=[(pool, sub_p)] if k == 0 else [],
                        inc=pool if k == NW - 1 else None)
            ndve_done[tt] = dve.n
            npool_done[tt] = pool.n

        ldc[0] = self.ld("sync", dx[0], xfv(0), tile(0))
        for tt in range(NT):
            b = tt % 2
            p.emit("vector", lambda e, b=b: e.tensor_copy(xb[b][:, :], xf[b][:, :]),
                   wait=[(dx[b], ldc[tt]), (pool, npool_done.get(tt - 2, 0)),
                         (pe, sum_done.get(tt - 1, 0))],
                   inc=dve)
            cast_done[tt] = dve.n
            p.emit("scalar", lambda e, b=b: e.activation(
                xq[b][:, :], xf[b][:, :], AF.Square),
                wait=[(dx[b], ldc[tt]), (pe, stats_done.get(tt - 1, 0))],
                inc=act)
            sq_done[tt] = act.n
            pst = ps[b]
            for ic in range(IC):
                p.emit("tensor", lambda e, b=b, ic=ic, pst=pst: e.matmul(
                    pst[:, 0:TT], self.ones[:, :],
                    xb[b][:, ic * TT:(ic + 1) * TT],
                    start=(ic == 0), stop=(ic == IC - 1)),
                    wait=([(dve, cast_done[tt]), (dve, fin_done.get(tt - 2, 0))]
                          if ic == 0 else []),
                    inc=pe if ic == IC - 1 else None)
            sum_done[tt] = pe.n
            for ic in range(IC):
                p.emit("tensor", lambda e, b=b, ic=ic, pst=pst: e.matmul(
                    pst[:, TT:2 * TT], self.ones[:, :],
                    xq[b][:, ic * TT:(ic + 1) * TT],
                    start=(ic == 0), stop=(ic == IC - 1)),
                    wait=[(act, sq_done[tt])] if ic == 0 else [],
                    inc=pe if ic == IC - 1 else None)
            stats_done[tt] = pe.n
            if tt >= 1:
                fin_norm(tt - 1)
            if tt + 1 < NT:
                w = []
                if tt >= 1:
                    w = [(dve, ndve_done.get(tt - 1, 0)),
                         (act, sq_done[tt - 1]),
                         (pool, npool_done.get(tt - 1, 0))]
                ldc[tt + 1] = self.ld("sync", dx[(tt + 1) % 2],
                                      xfv((tt + 1) % 2), tile(tt + 1), wait=w)
        fin_norm(NT - 1)
        self.flush("ln")
        st.close()


    def ffn1(self):
        nc, p, s, t = self.nc, self.p, self.s, self.t
        call = self.ffn_call
        st = ExitStack()
        wgb = [self.sb(st, f"f1wg{i}", [128, D], BF16)
               for i in range(2)]
        wub = [self.sb(st, f"f1wu{i}", [128, D], BF16)
               for i in range(2)]
        gbuf = self.sb(st, "f1g", [128, S], F32)
        ubuf = self.sb(st, "f1u", [128, S], F32)
        gout = [self.sb(st, f"f1o{i}", [128, S], BF16)
                for i in range(2)]
        psg = self.pt(st, "f1pg", [128, S], F32)
        psu = self.pt(st, "f1pu", [128, S], F32)
        pe, act, dve = (s[k] for k in ("pe", "act", "dve"))
        dw, ds = self.dw, self.ds

        use_prep = call > 0
        wg_src = t["wg_bf"] if use_prep else t["wg"]
        wu_src = t["wu_bf"] if use_prep else t["wu"]
        q = "gpsimd"
        wsem = self.dg

        wld, mm_pcg, evac_pcg, mult_done, st_done = {}, {}, {}, {}, {}
        prep_st = {}

        def load_h(h):
            w = [(pe, mm_pcg.get((h - 2, 7), 0))]
            if not use_prep and h >= 2:
                w.append((ds[h % 2], prep_st.get(h - 2, 0)))
            self.ld(q, wsem[h % 2], wgb[h % 2][:, :],
                    wg_src[h * 128:(h + 1) * 128, :], w)
            wld[h] = self.ld(q, wsem[h % 2], wub[h % 2][:, :],
                             wu_src[h * 128:(h + 1) * 128, :])

        load_h(0)
        load_h(1)
        for h in range(HC):
            b = h % 2
            for j in range(8):
                proj, cg = divmod(j, 4)
                dst = psg if proj == 0 else psu
                wb = wgb if proj == 0 else wub
                for ic in range(IC):
                    w = []
                    if ic == 0:
                        w = [(wsem[h % 2], wld[h]),
                             (act, evac_pcg.get((h - 1, j), 0))]
                    p.emit("tensor", lambda e, b=b, cg=cg, ic=ic, dst=dst, wb=wb:
                           e.matmul(dst[:, cg * TT:(cg + 1) * TT],
                                    wb[b][:, ic * 128:(ic + 1) * 128],
                                    self.ln_x[:, ic * S + cg * TT:
                                              ic * S + cg * TT + TT],
                                    start=(ic == 0), stop=(ic == IC - 1)),
                           wait=w, inc=pe if ic == IC - 1 else None)
                mm_pcg[(h, j)] = pe.n
            for j in range(8):
                proj, cg = divmod(j, 4)
                src_ps = psg if proj == 0 else psu
                dst_sb = gbuf if proj == 0 else ubuf
                fn = AF.Silu if proj == 0 else AF.Copy
                p.emit("scalar", lambda e, cg=cg, src_ps=src_ps, dst_sb=dst_sb,
                       fn=fn: e.activation(
                           dst_sb[:, cg * TT:(cg + 1) * TT],
                           src_ps[:, cg * TT:(cg + 1) * TT], fn),
                       wait=[(pe, mm_pcg[(h, j)]),
                             (dve, mult_done.get(h - 1, 0) if j == 0 else 0)],
                       inc=act)
                evac_pcg[(h, j)] = act.n
            gb = gout[b]
            p.emit("vector", lambda e, gb=gb: e.tensor_mul(
                gb[:, :], gbuf[:, :], ubuf[:, :]),
                wait=[(act, evac_pcg[(h, 7)]), (ds[h % 2], st_done.get(h - 2, 0))],
                inc=dve)
            mult_done[h] = dve.n
            gdst = t["gated"].rearrange("p (q h t) -> h p q t", h=HC, t=TT)[h]
            st_done[h] = self.stq(ds[h % 2], gdst,
                                  gb[:, :].rearrange("p (q t) -> p q t", t=TT),
                                  [(dve, mult_done[h])])
            if not use_prep:
                self.stq(ds[h % 2], t["wg_bf"][h * 128:(h + 1) * 128, :],
                         wgb[b][:, :], [(wsem[h % 2], wld[h])])
                prep_st[h] = self.stq(ds[h % 2],
                                      t["wu_bf"][h * 128:(h + 1) * 128, :],
                                      wub[b][:, :], [])
                st_done[h] = prep_st[h]
            if h + 2 < HC:
                load_h(h + 2)
        self.flush(f"ffn1_{call}")
        st.close()

    def proj_scratch(self, name, rhs_dram, ncc, wsrc, wbf, mode, dst, src,
                     bias_ap=None, src_row0=0, dst_row0=0):
        nc, p, s, t = self.nc, self.p, self.s, self.t
        st = ExitStack()
        gres = self.sb(st, "pj_g", [128, ncc * TT], BF16)
        NBUF = 3 if (ncc <= 16 and mode == "use") else 2
        wdb = [self.sb(st, f"pj_w{i}", [128, 2 * ncc * 128], BF16)
               for i in range(NBUF)]
        xres = self.sb(st, "pj_x", [128, IC * TT], F32)
        obuf = [self.sb(st, f"pj_o{i}", [128, 2 * TT], F32)
                for i in range(2)]
        pso = self.pt(st, "pj_ps", [128, 4 * TT], F32)
        pe, dve = s["pe"], s["dve"]
        dx, dw, ds = self.dx, self.dw, self.ds
        NE = 8
        WB = ncc * 128
        NQ = 4
        NL = NT * NE
        mm_done, ep_done, ost_done = {}, {}, {}
        wld_map, pst_done = {}, {}

        def emit_wload(l):
            ttp, e8p = divmod(l, NE)
            bb = l % NBUF
            sb2 = e8p % 2
            use_bf = (mode == "use") or (mode == "first" and ttp > 0)
            ws = wbf if use_bf else wsrc
            q = "sync" if use_bf else "gpsimd"
            lsem = dw[sb2] if use_bf else self.dg[sb2]
            wpre = [(pe, mm_done.get(l - 2, 0)),
                    (ds[sb2], ost_done.get(l - 2, 0))]
            if mode == "first" and ttp == 1 and e8p == 0:
                wpre += [(ds[0], ds[0].n), (ds[1], ds[1].n)]
            for oo in range(2):
                wld_map[l] = (lsem, self.ld(
                    q, lsem, wdb[bb][:, oo * WB:(oo + 1) * WB],
                    ws[(e8p * 2 + oo) * 128:(e8p * 2 + oo + 1) * 128, :],
                    wpre if oo == 0 else ()))
            if mode == "first" and ttp == 0:
                for oo in range(2):
                    pst_done[l] = self.stq(
                        ds[sb2],
                        wbf[(e8p * 2 + oo) * 128:
                            (e8p * 2 + oo + 1) * 128, :],
                        wdb[bb][:, oo * WB:(oo + 1) * WB],
                        [wld_map[l]])

        for li in range(NBUF - 1):
            emit_wload(li)
        for tt in range(NT):
            w_gres = [(pe, mm_done.get(tt * NE - 1, 0))]
            w_xres = [(dve, ep_done.get(tt * NE - 1, 0))]
            g_ld = {}
            qsz = ncc * TT // NQ
            for qq in range(NQ):
                g_ld[qq] = (self.dq[qq], self.ld(
                    "sync", self.dq[qq], gres[:, qq * qsz:(qq + 1) * qsz],
                    rhs_dram[:, tt * ncc * TT + qq * qsz:
                             tt * ncc * TT + (qq + 1) * qsz],
                    w_gres if qq == 0 else ()))
            x_ld = self.ld("sync", dx[tt % 2],
                           xres[:, :].rearrange("p (ic t) -> p ic t", t=TT),
                           src[src_row0:src_row0 + D,
                               tt * TT:(tt + 1) * TT].rearrange(
                               "(ic p) t -> p ic t", p=128), w_xres)
            for e8 in range(NE):
                l = tt * NE + e8
                b = e8 % 2
                wb_ = l % NBUF
                pb = b * 2
                hq = ncc // NQ
                for oo in range(2):
                    for h in range(ncc):
                        w = []
                        if h == 0 and oo == 0:
                            w = [wld_map[l], g_ld[0],
                                 (dve, ep_done.get(l - 2, 0))]
                        elif h % hq == 0 and e8 == 0 and oo == 0:
                            w = [g_ld[h // hq]]
                        p.emit("tensor", lambda e, wb_=wb_, h=h, oo=oo, pb=pb:
                               e.matmul(pso[:, (pb + oo) * TT:(pb + oo + 1) * TT],
                                        wdb[wb_][:, oo * WB + h * 128:
                                                 oo * WB + (h + 1) * 128],
                                        gres[:, h * TT:(h + 1) * TT],
                                        start=(h == 0), stop=(h == ncc - 1)),
                               wait=w,
                               inc=pe if (h == ncc - 1 and oo == 1) else None)
                mm_done[l] = pe.n
                if l + NBUF - 1 < NL:
                    emit_wload(l + NBUF - 1)
                ob = obuf[b]
                for oo in range(2):
                    oc = e8 * 2 + oo
                    w = ([(pe, mm_done[l]), (ds[b], ost_done.get(l - 2, 0)),
                          (dx[tt % 2], x_ld)]
                         if oo == 0 else [])
                    if bias_ap is None:
                        p.emit("vector", lambda e, ob=ob, oo=oo, oc=oc, pb=pb:
                               e.tensor_add(
                                   ob[:, oo * TT:(oo + 1) * TT],
                                   pso[:, (pb + oo) * TT:(pb + oo + 1) * TT],
                                   xres[:, oc * TT:(oc + 1) * TT]),
                               wait=w, inc=dve if oo == 1 else None)
                    else:
                        p.emit("vector", lambda e, ob=ob, oo=oo, oc=oc, pb=pb:
                               e.scalar_tensor_tensor(
                                   ob[:, oo * TT:(oo + 1) * TT],
                                   pso[:, (pb + oo) * TT:(pb + oo + 1) * TT],
                                   bias_ap[:, oc:oc + 1],
                                   xres[:, oc * TT:(oc + 1) * TT],
                                   AluOpType.add, AluOpType.add),
                               wait=w, inc=dve if oo == 1 else None)
                ep_done[l] = dve.n
                for oo in range(2):
                    oc = e8 * 2 + oo
                    ost_done[l] = self.stq(
                        ds[b],
                        dst[dst_row0 + oc * 128: dst_row0 + (oc + 1) * 128,
                            tt * TT:(tt + 1) * TT],
                        ob[:, oo * TT:(oo + 1) * TT],
                        [(dve, ep_done[l])] if oo == 0 else ())
        self.flush(f"pj_{name}_{self._uid}")
        st.close()


    def rope_tables(self):
        nc, p, s = self.nc, self.p, self.s
        import math
        st = ExitStack()
        i2 = self.sb(st, "rp_i2", [128, 2 * S], I32)
        trow = self.sb(st, "rp_t", [128, S], F32)
        ksm = self.sb(st, "rp_ks", [128, 2], I32)
        ivf = self.sb(st, "rp_v", [128, 2], F32)
        ivfT = [self.sb(st, f"rp_vt{j}", [1, 128], F32) for j in range(2)]
        identf = self.sb(st, "rp_idf", [128, 128], F32)
        fr = self.sb(st, "rp_f", [128, 2 * S], F32)
        kR = self.sb(st, "rp_r", [128, 2 * S], F32)
        rr = self.sb(st, "rp_rr", [128, 2 * S], F32)
        PS = self.pt(st, "rp_ps", [128, 2 * TT], F32)
        PT = self.pt(st, "rp_pt", [128, 128], F32)
        pe, act, dve, pool = s["pe"], s["act"], s["dve"], s["pool"]
        TAU = 2.0 * math.pi
        C1 = 12868.0 / 2048.0
        C2 = float(np.float32(TAU - C1))
        C3 = float(TAU - C1 - C2)

        p.emit("gpsimd", lambda e: e.iota(i2[0:1, 0:S], [[1, S]],
                                          channel_multiplier=0))
        p.emit("gpsimd", lambda e: e.iota(ksm[:, :], [[128, 2]],
                                          channel_multiplier=1))
        p.emit("gpsimd", lambda e: e.tensor_copy(identf[:, :],
                                                 self.ident[:, :]),
               inc=pool)
        ic0 = pool.n
        p.emit("vector", lambda e: e.tensor_copy(trow[0:1, :], i2[0:1, 0:S]),
               wait=[(pool, ic0)])
        p.emit("vector", lambda e: e.tensor_copy(ivf[:, :], ksm[:, :]),
               inc=dve)
        cp0 = dve.n
        p.emit("scalar", lambda e: e.activation(
            ivf[:, :], ivf[:, :], AF.Exp, scale=-math.log(10000.0) / 256.0),
            wait=[(dve, cp0)], inc=act)
        iv0 = act.n
        tpc = {}
        for j in range(2):
            p.emit("tensor", lambda e, j=j: e.transpose(
                PT[0:1, 0:128], ivf[:, j:j + 1], identf[:, :]),
                wait=[(act, iv0), (dve, tpc.get(j - 1, 0))], inc=pe)
            tp = pe.n
            p.emit("vector", lambda e, j=j: e.tensor_copy(
                ivfT[j][0:1, :], PT[0:1, 0:128]),
                wait=[(pe, tp)], inc=dve)
            tpc[j] = dve.n
        ivt = dve.n
        mmc, cpc = {}, {}
        for j in range(2):
            for cg in range(NT):
                i = j * NT + cg
                pb = (i % 2) * TT
                p.emit("tensor", lambda e, j=j, cg=cg, pb=pb: e.matmul(
                    PS[:, pb:pb + TT], ivfT[j][0:1, :],
                    trow[0:1, cg * TT:(cg + 1) * TT],
                    start=True, stop=True),
                    wait=[(dve, ivt if i < 2 else cpc.get(i - 2, 0))],
                    inc=pe)
                mmc[i] = pe.n
                dst = fr[:, j * S + cg * TT: j * S + (cg + 1) * TT]
                p.emit("vector", lambda e, dst=dst, pb=pb: e.tensor_copy(
                    dst, PS[:, pb:pb + TT]),
                    wait=[(pe, mmc[i])], inc=dve)
                cpc[i] = dve.n
        for phase, dstT in ((0.0, self.SIN), (0.5 * math.pi, self.COS)):
            p.emit("vector", lambda e, phase=phase: e.tensor_scalar(
                kR[:, :], fr[:, :], 1.0 / TAU, phase / TAU,
                AluOpType.mult, AluOpType.add),
                wait=[(dve, dve.n), (act, act.n)], inc=dve)
            p.emit("vector", lambda e: e.tensor_copy(i2[:, :], kR[:, :]),
                   wait=[(dve, dve.n)], inc=dve)
            p.emit("vector", lambda e: e.tensor_copy(kR[:, :], i2[:, :]),
                   wait=[(dve, dve.n)], inc=dve)
            p.emit("vector", lambda e: e.scalar_tensor_tensor(
                rr[:, :], kR[:, :], -C1, fr[:, :],
                AluOpType.mult, AluOpType.add), wait=[(dve, dve.n)], inc=dve)
            p.emit("vector", lambda e: e.scalar_tensor_tensor(
                rr[:, :], kR[:, :], -C2, rr[:, :],
                AluOpType.mult, AluOpType.add), wait=[(dve, dve.n)], inc=dve)
            p.emit("vector", lambda e: e.scalar_tensor_tensor(
                rr[:, :], kR[:, :], -C3, rr[:, :],
                AluOpType.mult, AluOpType.add), wait=[(dve, dve.n)], inc=dve)
            if phase != 0.0:
                p.emit("vector", lambda e, phase=phase: e.tensor_scalar_add(
                    rr[:, :], rr[:, :], phase), wait=[(dve, dve.n)], inc=dve)
            p.emit("vector", lambda e: e.tensor_scalar(
                kR[:, :], rr[:, :], math.pi, None, AluOpType.is_gt),
                wait=[(dve, dve.n)], inc=dve)
            p.emit("vector", lambda e: e.scalar_tensor_tensor(
                rr[:, :], kR[:, :], -TAU, rr[:, :],
                AluOpType.mult, AluOpType.add), wait=[(dve, dve.n)], inc=dve)
            p.emit("vector", lambda e: e.tensor_scalar(
                kR[:, :], rr[:, :], -math.pi, None, AluOpType.is_lt),
                wait=[(dve, dve.n)], inc=dve)
            p.emit("vector", lambda e: e.scalar_tensor_tensor(
                rr[:, :], kR[:, :], TAU, rr[:, :],
                AluOpType.mult, AluOpType.add), wait=[(dve, dve.n)], inc=dve)
            p.emit("vector", lambda e: e.tensor_scalar_min(
                rr[:, :], rr[:, :], math.pi), wait=[(dve, dve.n)], inc=dve)
            p.emit("vector", lambda e: e.tensor_scalar_max(
                rr[:, :], rr[:, :], -math.pi), wait=[(dve, dve.n)], inc=dve)
            rdone = dve.n
            p.emit("scalar", lambda e, dstT=dstT: e.activation(
                dstT[:, :], rr[:, :], AF.Sin), wait=[(dve, rdone)], inc=act)
        self.flush("rope")
        st.close()

    def attn_b_lats(self):
        nc, p, s, t = self.nc, self.p, self.s, self.t
        st = ExitStack()
        wl = self.sb(st, "ab_wl", [128, 2 * LC * D], BF16)
        obuf = [self.sb(st, f"ab_lo{i}", [128, TT], BF16) for i in range(2)]
        PA = self.pt(st, "ab_pa", [128, 2048], F32)
        pe, dve = s["pe"], s["dve"]
        dg, ds = self.dg, self.ds
        ld0 = self.ld("gpsimd", dg[0], wl[:, 0:LC * D].rearrange(
            "p (oc m) -> p oc m", m=D), t["wqd"][:, :].rearrange(
            "(oc q) m -> q oc m", q=128))
        ld0 = self.ld("gpsimd", dg[0], wl[:, LC * D:].rearrange(
            "p (oc m) -> p oc m", m=D), t["wkvd"][:, :].rearrange(
            "(oc q) m -> q oc m", q=128))
        ev, stc = {}, {}
        for pj, (dst_d, bn) in enumerate(
                [(t["qlat_d"], "bqd"), (t["kvlat_d"], "bkvd")]):
            bp = self.bias[bn]
            for oc in range(LC):
                for cg in range(NT):
                    u = pj * 16 + oc * NT + cg
                    bk = u % 4
                    ub = u % 2
                    for ic in range(IC):
                        w = []
                        if ic == 0:
                            w = [(dg[0], ld0), (dve, ev.get(u - 4, 0))]
                        p.emit("tensor", lambda e, pj=pj, oc=oc, cg=cg, ic=ic,
                               bk=bk: e.matmul(
                                   PA[:, bk * TT:(bk + 1) * TT],
                                   wl[:, pj * LC * D + oc * D + ic * 128:
                                      pj * LC * D + oc * D + (ic + 1) * 128],
                                   self.ln_x[:, ic * S + cg * TT:
                                             ic * S + cg * TT + TT],
                                   start=(ic == 0), stop=(ic == IC - 1)),
                               wait=w, inc=pe if ic == IC - 1 else None)
                    mm = pe.n
                    p.emit("vector", lambda e, ub=ub, bk=bk, oc=oc, bp=bp:
                           e.tensor_scalar_add(
                               obuf[ub][:, :], PA[:, bk * TT:(bk + 1) * TT],
                               bp[:, oc:oc + 1]),
                           wait=[(pe, mm), (ds[ub], stc.get(u - 2, 0))],
                           inc=dve)
                    ev[u] = dve.n
                    stc[u] = self.stq(
                        ds[ub],
                        dst_d[:, (oc * NT + cg) * TT:(oc * NT + cg + 1) * TT],
                        obuf[ub][:, :], [(dve, ev[u])])
        self.flush("ab_lats")
        st.close()

    def attn_b_ups(self, qhalf, khalf, kvlat):
        nc, p, s, t = self.nc, self.p, self.s, self.t
        st = ExitStack()
        wu2 = self.sb(st, "ab_wu", [128, 2 * 8 * LAT], BF16)
        qlat = self.sb(st, "ab_ql", [128, LC * S], BF16)
        PA = self.pt(st, "ab_pa2", [128, 2048], F32)
        pe, dve = s["pe"], s["dve"]
        dg, dx = self.dg, self.dx
        ld0 = self.ld("gpsimd", dg[0], wu2[:, 0:8 * LAT].rearrange(
            "p (oc m) -> p oc m", m=LAT), t["wqu"][:, :].rearrange(
            "(oc q) m -> q oc m", q=128))
        ld0 = self.ld("gpsimd", dg[0], wu2[:, 8 * LAT:].rearrange(
            "p (oc m) -> p oc m", m=LAT), t["wku"][:, :].rearrange(
            "(oc q) m -> q oc m", q=128))
        lx = self.ld("sync", dx[0], qlat[:, :], t["qlat_d"][:, :])
        lx = self.ld("sync", dx[0], kvlat[:, :], t["kvlat_d"][:, :])
        ev = {}
        for pj, (src_t, dst_t, bn) in enumerate(
                [(qlat, qhalf, "bqu"), (kvlat, khalf, "bku")]):
            bp = self.bias[bn]
            for oc in range(8):
                for cg in range(NT):
                    u = pj * 32 + oc * NT + cg
                    bk = u % 4
                    for lc in range(LC):
                        w = []
                        if lc == 0:
                            w = [(dg[0], ld0), (dx[0], lx),
                                 (dve, ev.get(u - 4, 0))]
                        p.emit("tensor", lambda e, pj=pj, oc=oc, cg=cg, lc=lc,
                               bk=bk, src_t=src_t: e.matmul(
                                   PA[:, bk * TT:(bk + 1) * TT],
                                   wu2[:, pj * 8 * LAT + oc * LAT + lc * 128:
                                       pj * 8 * LAT + oc * LAT + (lc + 1) * 128],
                                   src_t[:, lc * S + cg * TT:
                                         lc * S + cg * TT + TT],
                                   start=(lc == 0), stop=(lc == LC - 1)),
                               wait=w, inc=pe if lc == LC - 1 else None)
                    mm = pe.n
                    p.emit("vector", lambda e, dst_t=dst_t, oc=oc, cg=cg,
                           bk=bk, bp=bp: e.tensor_scalar_add(
                               dst_t[:, oc * S + cg * TT:
                                     oc * S + (cg + 1) * TT],
                               PA[:, bk * TT:(bk + 1) * TT],
                               bp[:, oc:oc + 1]),
                           wait=[(pe, mm)], inc=dve)
                    ev[u] = dve.n
        self.flush("ab_ups")
        st.close()

    def attn_b_heads(self, qhalf, khalf, kvlat):
        nc, p, s, t = self.nc, self.p, self.s, self.t
        st = ExitStack()
        wvb = [self.sb(st, f"ab_wv{i}", [128, LAT], BF16) for i in range(2)]
        qrh = self.sb(st, "ab_qr", [128, S], BF16)
        krh = self.sb(st, "ab_kr", [128, S], BF16)
        tmx = self.sb(st, "ab_tm", [128, S], BF16)
        vfm = self.sb(st, "ab_vf", [128, S], BF16)
        vtm = self.sb(st, "ab_vt", [128, S], BF16)
        sqq = self.sb(st, "ab_sq", [128, S], BF16)
        nrm = self.sb(st, "ab_nr", [128, 4], F32)
        prT = self.sb(st, "ab_pr", [128, IC * TT], BF16)
        rcp = self.sb(st, "ab_rc", [128, TT], BF16)
        ob = [self.sb(st, f"ab_o{i}", [128, S], BF16) for i in range(2)]
        PA = self.pt(st, "ab_pa3", [128, 2048], F32)
        PB = self.pt(st, "ab_pb", [128, 1024], F32)
        PAV = self.pt(st, "ab_pav", [128, 512], F32)
        PT = [self.pt(st, "ab_pt", [128, 1024], BF16)] * 2
        pe, act, dve = s["pe"], s["act"], s["dve"]
        dg, ds = self.dg, self.ds

        wld, vdone, ostore, mdone = {}, {}, {}, {}
        exdone = {}
        lastmul = {}
        av_prev2 = 0

        def load_v(h):
            b = h % 2
            wld[h] = self.ld("gpsimd", dg[b], wvb[b][:, :],
                             t["wvu"][h * 128:(h + 1) * 128, :],
                             [(pe, vdone.get(h - 2, 0))])

        load_v(0)
        load_v(1)
        DBG_SKIPV = getattr(self, "dbg_skipv", False)
        for h in range(NH):
            b = h % 2
            if getattr(self, "dbg_serpre", False) and h > 0:
                p.emit("tensor", lambda e: e.nop(),
                       wait=[(act, act.n), (dve, dve.n), (s["pool"], s["pool"].n)])
                p.emit("vector", lambda e: e.nop(),
                       wait=[(pe, pe.n), (act, act.n)])
                p.emit("scalar", lambda e: e.nop(),
                       wait=[(pe, pe.n), (dve, dve.n)])
            if h < 8:
                q_h = qhalf[:, h * S:(h + 1) * S]
                k_h = khalf[:, h * S:(h + 1) * S]
                rope_done = 0
            else:
                j = h - 8
                jj = j if j < 4 else j - 4
                c = jj % 2
                x1q = qhalf[:, jj * S:(jj + 1) * S]
                x2q = qhalf[:, (jj + 4) * S:(jj + 5) * S]
                x1k = khalf[:, jj * S:(jj + 1) * S]
                x2k = khalf[:, (jj + 4) * S:(jj + 5) * S]
                CS = self.COS[:, c * S:(c + 1) * S]
                SN = self.SIN[:, c * S:(c + 1) * S]
                if j < 4:
                    combos = [(x1q, CS, x2q, SN, AluOpType.subtract, qrh),
                              (x1k, CS, x2k, SN, AluOpType.subtract, krh)]
                else:
                    combos = [(x1q, SN, x2q, CS, AluOpType.add, qrh),
                              (x1k, SN, x2k, CS, AluOpType.add, krh)]
                for a1, c1, a2, c2, op, dstr in combos:
                    p.emit("vector", lambda e, a1=a1, c1=c1: e.tensor_mul(
                        tmx[:, :], a1, c1),
                        wait=[(dve, dve.n),
                              (pe, mdone.get(h - 1, 0))], inc=dve)
                    t1 = dve.n
                    p.emit("vector", lambda e, a2=a2, c2=c2, dstr=dstr:
                           e.tensor_mul(dstr[:, :], a2, c2),
                           wait=[(dve, t1)], inc=dve)
                    t2 = dve.n
                    p.emit("vector", lambda e, op=op, dstr=dstr:
                           e.tensor_tensor(dstr[:, :], tmx[:, :], dstr[:, :],
                                           op),
                           wait=[(dve, t2)], inc=dve)
                q_h, k_h = qrh, krh
                rope_done = dve.n
            nq_done = {}
            for i, zz in enumerate((q_h, k_h)):
                p.emit("scalar", lambda e, zz=zz: e.activation(
                    sqq[:, :], zz[:, :], AF.Square),
                    wait=[(dve, rope_done),
                          (pe, nq_done.get(i - 1, 0)),
                          (act, self.ab_rcd.get(h - 1, 0))], inc=act)
                sq_d = act.n
                for cg in range(NT):
                    w = [(act, sq_d), (dve, self.ab_nbg)]
                    if cg == 0 and i == 0:
                        w.append((dve, self.ab_nrm.get(h - 1, 0)))
                    p.emit("tensor", lambda e, cg=cg: e.matmul(
                        PB[:, 0:TT], self.ones[:, :],
                        sqq[:, cg * TT:(cg + 1) * TT],
                        start=True, stop=True),
                        wait=w, inc=pe)
                    nmm = pe.n
                    p.emit("vector", lambda e, cg=cg, i=i: e.reduce_max(
                        nrm[:, 3:4], PB[:, 0:TT], axis=AX),
                        wait=([(pe, nmm), (dve, dve.n),
                               (act, self.ab_rcd.get(h - 1, 0))]
                              if (i == 0 and cg == 0) else
                              [(pe, nmm), (dve, dve.n)]), inc=dve)
                    rmx = dve.n
                    if cg == 0:
                        p.emit("vector", lambda e, i=i: e.tensor_copy(
                            nrm[:, i:i + 1], nrm[:, 3:4]),
                            wait=[(dve, rmx)], inc=dve)
                    else:
                        p.emit("vector", lambda e, i=i: e.tensor_max(
                            nrm[:, i:i + 1], nrm[:, i:i + 1], nrm[:, 3:4]),
                            wait=[(dve, rmx)], inc=dve)
                    self.ab_nbg = dve.n
                nq_done[i] = pe.n
            p.emit("vector", lambda e: e.tensor_mul(
                nrm[:, 2:3], nrm[:, 0:1], nrm[:, 1:2]),
                wait=[(dve, dve.n)], inc=dve)
            nprod = dve.n
            p.emit("scalar", lambda e: e.activation(
                nrm[:, 2:3], nrm[:, 2:3], AF.Ln, bias=1e-30),
                wait=[(dve, nprod)], inc=act)
            lnn = act.n
            p.emit("scalar", lambda e: e.activation(
                nrm[:, 2:3], nrm[:, 2:3], AF.Exp, scale=0.5),
                wait=[(act, lnn)], inc=act)
            mdn = act.n
            p.emit("vector", lambda e: e.tensor_scalar_mul(
                nrm[:, 3:4], nrm[:, 2:3], -ISQ),
                wait=[(act, mdn)], inc=dve)
            bias_done = dve.n
            if DBG_SKIPV and h > 0:
                pass
            else:
             for cg in range(NT):
                for lc in range(LC):
                    w = []
                    if lc == 0 and cg == 0:
                        w = [(dg[b], wld[h]), (dve, lastmul.get(h - 1, 0))]
                    p.emit("tensor", lambda e, b=b, cg=cg, lc=lc: e.matmul(
                        PA[:, cg * TT:(cg + 1) * TT],
                        wvb[b][:, lc * 128:(lc + 1) * 128],
                        kvlat[:, lc * S + cg * TT: lc * S + cg * TT + TT],
                        start=(lc == 0), stop=(lc == LC - 1)),
                        wait=w, inc=pe if lc == LC - 1 else None)
                mmv = pe.n
                p.emit("vector", lambda e, cg=cg, h=h: e.tensor_scalar_add(
                    vfm[:, cg * TT:(cg + 1) * TT],
                    PA[:, cg * TT:(cg + 1) * TT],
                    self.bias["bvu"][:, h:h + 1]),
                    wait=[(pe, mmv)], inc=dve if cg == NT - 1 else None)
            vev = dve.n
            for g in range(2):
                pt = PT[g]
                for c8 in range(8):
                    c = g * 8 + c8
                    w = []
                    if c8 == 0:
                        prevu = (self.ab_te.get((h, 0), 0) if g == 1
                                 else self.ab_te.get((h - 1, 1), 0))
                        w = [(dve, vev), (dve, prevu)]
                    p.emit("tensor", lambda e, pt=pt, c8=c8, c=c: e.transpose(
                        pt[:, c8 * 128:(c8 + 1) * 128],
                        vfm[:, c * 128:(c + 1) * 128], self.ident[:, :]),
                        wait=w, inc=pe if c8 == 7 else None)
                tp = pe.n
                for c8 in range(8):
                    c = g * 8 + c8
                    p.emit("vector", lambda e, pt=pt, c8=c8, c=c:
                           e.tensor_copy(vtm[:, c * 128:(c + 1) * 128],
                                         pt[:, c8 * 128:(c8 + 1) * 128]),
                           wait=[(pe, tp)] if c8 == 0 else [],
                           inc=dve if c8 == 7 else None)
                self.ab_te[(h, g)] = dve.n
            vdone[h] = pe.n
            vtm_done = dve.n
            mul_prev = self.ab_mul.get(h - 1, 0)
            DBG_ST4 = 0 if (getattr(self, "dbg_skips", False) and h > 0) else NT
            for st4 in range(DBG_ST4):
                dpb = (st4 % 2) * TT
                exd = {}
                for skc in range(IC):
                    pb = (skc % 4) * TT
                    w = [(dve, bias_done)]
                    if skc >= 4:
                        w.append((act, exd[(skc - 4) // 2]))
                    elif st4 > 0:
                        w.append((act, exdone.get((h, st4 - 1), 0)))
                    else:
                        w.append((act, exdone.get((h - 1, NT - 1), 0)))
                    p.emit("tensor", lambda e, skc=skc, st4=st4, pb=pb,
                           k_h=k_h, q_h=q_h: e.matmul(
                               PA[:, pb:pb + TT],
                               k_h[:, skc * 128:(skc + 1) * 128],
                               q_h[:, st4 * TT:(st4 + 1) * TT],
                               start=True, stop=True),
                           wait=w, inc=pe)
                    smm = pe.n
                    if skc % 2 == 1:
                        j = (skc - 1) // 2
                        pb0 = ((skc - 1) % 4) * TT
                        p.emit("scalar", lambda e, skc=skc, pb0=pb0:
                               e.activation(
                                   prT[:, (skc - 1) * TT:(skc + 1) * TT],
                                   PA[:, pb0:pb0 + 2 * TT],
                                   AF.Exp, bias=nrm[:, 3:4], scale=ISQ),
                               wait=([(pe, smm), (pe, av_prev2)] if skc == 1
                                     else [(pe, smm)]), inc=act)
                        exd[j] = act.n
                        if j >= 2:
                            for kk in (skc - 5, skc - 4):
                                p.emit("tensor", lambda e, kk=kk, st4=st4,
                                       dpb=dpb: e.matmul(
                                           PB[:, dpb:dpb + TT],
                                           self.ones[:, :],
                                           prT[:, kk * TT:(kk + 1) * TT],
                                           start=(kk == 0), stop=False),
                                       wait=[(act, exd[j - 2]),
                                             (act, self.ab_lnd
                                              if kk == 0 else 0),
                                             (dve, mul_prev
                                              if kk == 0 else 0)])
                                p.emit("tensor", lambda e, kk=kk, st4=st4:
                                       e.matmul(PAV[:, 0:TT],
                                                vtm[:, kk * 128:(kk + 1) * 128],
                                                prT[:, kk * TT:(kk + 1) * TT],
                                                start=(kk == 0), stop=False),
                                       wait=[(dve, vtm_done
                                              if kk == 0 else 0),
                                             (dve, mul_prev
                                              if kk == 0 else 0)])
                exdone[(h, st4)] = act.n
                for kk in (IC - 4, IC - 3, IC - 2, IC - 1):
                    p.emit("tensor", lambda e, kk=kk, st4=st4, dpb=dpb:
                           e.matmul(PB[:, dpb:dpb + TT], self.ones[:, :],
                                    prT[:, kk * TT:(kk + 1) * TT],
                                    start=False, stop=(kk == IC - 1)),
                           wait=[(act, exd[kk // 2])] if kk % 2 == 0 else [],
                           inc=pe if kk == IC - 1 else None)
                    p.emit("tensor", lambda e, kk=kk, st4=st4:
                           e.matmul(PAV[:, 0:TT],
                                    vtm[:, kk * 128:(kk + 1) * 128],
                                    prT[:, kk * TT:(kk + 1) * TT],
                                    start=False, stop=(kk == IC - 1)),
                           inc=pe if kk == IC - 1 else None)
                avm = pe.n
                av_prev2 = avm
                self.ab_dn[(h, st4)] = pe.n
                p.emit("scalar", lambda e, st4=st4, dpb=dpb: e.activation(
                    sqq[:, 0:TT], PB[:, dpb:dpb + TT], AF.Ln, bias=1e-30),
                    wait=[(pe, avm), (dve, dve.n)], inc=act)
                lnd = act.n
                self.ab_lnd = lnd
                p.emit("scalar", lambda e: e.activation(
                    rcp[:, :], sqq[:, 0:TT], AF.Exp, scale=-1.0),
                    wait=[(act, lnd), (dve, mul_prev)], inc=act)
                rcd = act.n
                p.emit("vector", lambda e, st4=st4, ob=ob[b]: e.tensor_mul(
                    ob[:, st4 * TT:(st4 + 1) * TT], PAV[:, 0:TT],
                    rcp[:, :]),
                    wait=[(pe, avm), (act, rcd),
                          (ds[b], ostore.get(h - 2, 0))], inc=dve)
                mul_prev = dve.n
                lastmul[h] = dve.n
            self.ab_mul[h] = lastmul.get(h, mul_prev)
            self.ab_rcd[h] = act.n
            self.ab_nrm[h] = lastmul.get(h, mul_prev)
            mdone[h] = pe.n

            adst = t["aout"].rearrange("p (q c t) -> c p q t", c=IC, t=TT)[h]
            ostore[h] = self.stq(
                ds[b], adst, ob[b][:, :].rearrange("p (q t) -> p q t", t=TT),
                [(dve, lastmul.get(h, 0))])
            if h + 2 < NH:
                load_v(h + 2)
            if getattr(self, "dbg_headbar", False):
                self.flush(f"abh_{h}")
            if getattr(self, "dbg_headbar2", False):
                self.p.flush(f"abh2_{h}")
        self.flush("attn_b")
        st.close()

    def attn_b(self, src, dst, seq, src_row0=0, dst_row0=0):
        lnsub = ExitStack()
        self.ln_x = self.sb(lnsub, "ln_x", [128, IC * S], BF16)
        self.ln(src, src_row0)
        self.attn_b_lats()
        lnsub.close()
        sub = ExitStack()
        qhalf = self.sb(sub, "ab_qh", [128, 8 * S], BF16)
        khalf = self.sb(sub, "ab_kh", [128, 8 * S], BF16)
        kvlat = self.sb(sub, "ab_kl", [128, LC * S], BF16)
        self.attn_b_ups(qhalf, khalf, kvlat)
        self.ab_te, self.ab_rcd, self.ab_nrm, self.ab_mul = {}, {}, {}, {}
        self.ab_dn = {}
        self.ab_nbg = 0
        self.ab_lnd = 0
        self.attn_b_heads(qhalf, khalf, kvlat)
        sub.close()
        self.proj_scratch("wob", self.t["aout"], IC, self.t["wob"],
                          self.t["wob_bf"], "first" if seq == 0 else "use",
                          dst, src, self.bias["bob"], src_row0, dst_row0)

    def attn_a_heads(self):
        nc, p, s, t = self.nc, self.p, self.s, self.t
        st = ExitStack()
        whb = [self.sb(st, f"aa_w{i}", [128, 3 * D], BF16) for i in range(2)]
        e1 = self.sb(st, "aa_e1", [128, S], F32)
        e2 = self.sb(st, "aa_e2", [128, S], F32)
        qh = self.sb(st, "aa_q", [128, S], BF16)
        kfm = self.sb(st, "aa_kf", [128, S], BF16)
        vfm = self.sb(st, "aa_vf", [128, S], BF16)
        ktm = self.sb(st, "aa_kt", [128, S], BF16)
        vtm = self.sb(st, "aa_vt", [128, S], BF16)
        kvb = self.sb(st, "aa_kv", [128, 128], BF16)
        ksc = self.sb(st, "aa_ks", [128, 1], F32)
        ksb = self.sb(st, "aa_kb", [128, 128], BF16)
        plane = self.sb(st, "aa_pl", [128, S], BF16)
        ob = [self.sb(st, f"aa_o{i}", [128, S], BF16) for i in range(2)]
        PA = self.pt(st, "aa_pa", [128, 2048], F32)
        PB = self.pt(st, "aa_pb", [128, 1024], F32)
        PT = [self.pt(st, f"aa_pt{i}", [128, 1024], BF16) for i in range(2)]
        pe, act, dve = s["pe"], s["act"], s["dve"]
        dg, ds = self.dg, self.ds

        wld = {}
        self.aa_lastln = {}
        head_end = {}
        mult_done = {}
        kv_evac = {}
        vte_done = {}
        ostore = {}

        def load_h(h):
            b = h % 2
            w = [(pe, head_end.get(h - 2, 0))]
            for j, wn in enumerate(("wq", "wk", "wv")):
                wld[h] = self.ld("gpsimd", dg[b], whb[b][:, j * D:(j + 1) * D],
                                 t[wn][h * 128:(h + 1) * 128, :],
                                 w if j == 0 else ())

        load_h(0)
        load_h(1)
        for h in range(NH):
            b = h % 2
            proj_mm = {}
            add_done = {}
            rel_done = {}
            for pj, (wn_off, bn) in enumerate(
                    [(0, "bq"), (1, "bk"), (2, "bv")]):
                for cg in range(NT):
                    for ic in range(IC):
                        w = []
                        if ic == 0:
                            w = [(dg[b], wld[h])]
                            if pj == 0:
                                w += [(dve, mult_done.get(h - 1, 0))]
                            else:
                                w += [(act, rel_done[pj - 1])]
                        p.emit("tensor", lambda e, b=b, cg=cg, ic=ic,
                               wo=wn_off: e.matmul(
                                   PA[:, cg * TT:(cg + 1) * TT],
                                   whb[b][:, wo * D + ic * 128:
                                          wo * D + (ic + 1) * 128],
                                   self.ln_x[:, ic * S + cg * TT:
                                             ic * S + cg * TT + TT],
                                   start=(ic == 0), stop=(ic == IC - 1)),
                               wait=w, inc=pe if ic == IC - 1 else None)
                    proj_mm[(pj, cg)] = pe.n
                if pj < 2:
                    bp = self.bias[bn]
                    bn_ = self.negb[bn]
                    for cg in range(NT):
                        p.emit("scalar", lambda e, cg=cg, bp=bp, h=h: e.activation(
                            e1[:, cg * TT:(cg + 1) * TT],
                            PA[:, cg * TT:(cg + 1) * TT], AF.Relu,
                            bias=bp[:, h:h + 1]),
                            wait=[(pe, proj_mm[(pj, cg)]),
                                  (dve, add_done.get(pj - 1,
                                                     mult_done.get(h - 1, 0)))]
                            if cg == 0 else [(pe, proj_mm[(pj, cg)])])
                        p.emit("scalar", lambda e, cg=cg, bn_=bn_, h=h: e.activation(
                            e2[:, cg * TT:(cg + 1) * TT],
                            PA[:, cg * TT:(cg + 1) * TT], AF.Relu,
                            bias=bn_[:, h:h + 1], scale=-1.0),
                            inc=act if cg == NT - 1 else None)
                    rel_done[pj] = act.n
                    for cg in range(NT):
                        p.emit("scalar", lambda e, cg=cg: e.activation(
                            e2[:, cg * TT:(cg + 1) * TT],
                            e2[:, cg * TT:(cg + 1) * TT], AF.Exp, scale=-1.0),
                            wait=[(act, rel_done[pj])] if cg == 0 else [],
                            inc=act if cg == NT - 1 else None)
                    exp_done = act.n
                    dstq = qh if pj == 0 else kfm
                    for cg in range(NT):
                        p.emit("vector", lambda e, cg=cg, dstq=dstq:
                               e.tensor_add(
                                   dstq[:, cg * TT:(cg + 1) * TT],
                                   e1[:, cg * TT:(cg + 1) * TT],
                                   e2[:, cg * TT:(cg + 1) * TT]),
                               wait=[(act, exp_done)] if cg == 0 else [],
                               inc=dve if cg == NT - 1 else None)
                    add_done[pj] = dve.n
                else:
                    bp = self.bias[bn]
                    for cg in range(NT):
                        p.emit("vector", lambda e, cg=cg, bp=bp, h=h:
                               e.tensor_scalar_add(
                                   vfm[:, cg * TT:(cg + 1) * TT],
                                   PA[:, cg * TT:(cg + 1) * TT],
                                   bp[:, h:h + 1]),
                               wait=[(pe, proj_mm[(pj, cg)])],
                               inc=dve if cg == NT - 1 else None)
                    rel_done[pj] = dve.n
            v_evac = rel_done[2]
            te_cnt = {}
            for src_t, dst_t, dep_sem, dep_val in (
                    (kfm, ktm, dve, add_done[1]), (vfm, vtm, dve, v_evac)):
                for g in range(2):
                    pt = PT[g]
                    for c8 in range(8):
                        c = g * 8 + c8
                        w = []
                        if c8 == 0:
                            w = [(dep_sem, dep_val),
                                 (dve, vte_done.get(h - 1, 0)
                                  if src_t is kfm else te_cnt.get(g, 0))]

                        p.emit("tensor", lambda e, pt=pt, c8=c8, c=c,
                               src_t=src_t: e.transpose(
                                   pt[:, c8 * 128:(c8 + 1) * 128],
                                   src_t[:, c * 128:(c + 1) * 128],
                                   self.ident[:, :]),
                               wait=w, inc=pe if c8 == 7 else None)
                    tp_done = pe.n
                    for c8 in range(8):
                        c = g * 8 + c8
                        p.emit("vector", lambda e, pt=pt, c8=c8, c=c,
                               dst_t=dst_t: e.tensor_copy(
                                   dst_t[:, c * 128:(c + 1) * 128],
                                   pt[:, c8 * 128:(c8 + 1) * 128]),
                               wait=[(pe, tp_done)] if c8 == 0 else [],
                               inc=dve if c8 == 7 else None)
                    te_cnt[g] = dve.n
            vte_done[h] = dve.n
            for c in range(IC):
                p.emit("tensor", lambda e, c=c: e.matmul(
                    PB[:, 0:128], ktm[:, c * 128:(c + 1) * 128],
                    vtm[:, c * 128:(c + 1) * 128],
                    start=(c == 0), stop=(c == IC - 1)),
                    wait=[(dve, vte_done[h]),
                          (act, kv_evac.get(h - 1, 0))] if c == 0 else [])
            for c in range(IC):
                p.emit("tensor", lambda e, c=c: e.matmul(
                    PB[:, 128:129], ktm[:, c * 128:(c + 1) * 128],
                    self.ones[:, 0:1],
                    start=(c == 0), stop=(c == IC - 1)),
                    inc=pe if c == IC - 1 else None)
            kvmm = pe.n
            p.emit("scalar", lambda e: e.activation(
                kvb[:, :], PB[:, 0:128], AF.Copy), wait=[(pe, kvmm)])
            p.emit("scalar", lambda e: e.activation(
                ksc[:, :], PB[:, 128:129], AF.Copy), inc=act)
            kv_evac[h] = act.n
            p.emit("vector", lambda e: e.tensor_scalar(
                ksb[:, :], self.ones[:, :], ksc[:, 0:1], None,
                AluOpType.mult),
                wait=[(act, kv_evac[h])], inc=dve)
            ksb_done = dve.n
            for cg in range(NT):
                p.emit("tensor", lambda e, cg=cg: e.matmul(
                    PA[:, cg * TT:(cg + 1) * TT], kvb[:, :],
                    qh[:, cg * TT:(cg + 1) * TT], start=True, stop=True),
                    wait=[(dve, ksb_done)] if cg == 0 else [],
                    inc=pe if cg == NT - 1 else None)
            qkv_mm = pe.n
            dn_mm, ln_dn = {}, {}
            for cg in range(NT):
                p.emit("tensor", lambda e, cg=cg: e.matmul(
                    PB[:, 512:1024], ksb[:, :],
                    qh[:, cg * TT:(cg + 1) * TT], start=True, stop=True),
                    wait=([(act, ln_dn.get(cg - 1, 0))] if cg > 0 else
                          [(dve, ksb_done), (act, self.aa_lastln.get(0, 0))]),
                    inc=pe)
                dn_mm[cg] = pe.n
                p.emit("scalar", lambda e, cg=cg: e.activation(
                    e2[:, cg * TT:(cg + 1) * TT], PB[:, 512:1024],
                    AF.Ln, bias=1e-8),
                    wait=[(pe, dn_mm[cg]), (dve, add_done[1])], inc=act)
                ln_dn[cg] = act.n
                p.emit("scalar", lambda e, cg=cg: e.activation(
                    plane[:, cg * TT:(cg + 1) * TT],
                    e2[:, cg * TT:(cg + 1) * TT], AF.Exp, scale=-1.0),
                    wait=[(act, ln_dn[cg])], inc=act)
            rc_done = act.n
            self.aa_lastln[0] = ln_dn[NT - 1]
            for cg in range(NT):
                p.emit("vector", lambda e, cg=cg, ob=ob[b]: e.tensor_mul(
                    ob[:, cg * TT:(cg + 1) * TT],
                    PA[:, cg * TT:(cg + 1) * TT],
                    plane[:, cg * TT:(cg + 1) * TT]),
                    wait=[(pe, qkv_mm), (act, rc_done),
                          (ds[b], ostore.get(h - 2, 0))] if cg == 0 else [],
                    inc=dve if cg == NT - 1 else None)
            mult_done[h] = dve.n
            head_end[h] = pe.n
            adst = t["aout"].rearrange("p (q c t) -> c p q t", c=IC, t=TT)[h]
            ostore[h] = self.stq(
                ds[b], adst, ob[b][:, :].rearrange("p (q t) -> p q t", t=TT),
                [(dve, mult_done[h])])
            if h + 2 < NH:
                load_h(h + 2)
        self.flush("attn_a")
        st.close()

    def attn_a(self, src, dst, seq, src_row0=0, dst_row0=0):
        sub = ExitStack()
        self.ln_x = self.sb(sub, "ln_x", [128, IC * S], BF16)
        self.ln(src, src_row0)
        self.attn_a_heads()
        sub.close()
        self.proj_scratch("wo", self.t["aout"], IC, self.t["wo"],
                          self.t["wo_bf"], "first" if seq == 0 else "use",
                          dst, src, self.bias["bo"], src_row0, dst_row0)

    def ffn(self, src, dst, src_row0=0, dst_row0=0):
        sub = ExitStack()
        self.ln_x = self.sb(sub, "ln_x", [128, IC * S], BF16)
        self.ln(src, src_row0)
        self.ffn1()
        sub.close()
        self.proj_scratch("wd", self.t["gated"], HC, self.t["wd"],
                          self.t["wd_bf"],
                          "first" if self.ffn_call == 0 else "use",
                          dst, src, None, src_row0, dst_row0)
        self.ffn_call += 1

    def finish(self):
        self.flush("fin")
        self.st.close()
        return self.nc


def _pack_inputs(inp):
    m = {}
    x = np.asarray(inp["L0_attn_a_x"])
    m["_x_full"] = np.ascontiguousarray(x.transpose(0, 2, 1))
    for k, nm in [("wq", "L0_attn_a_wq"), ("wk", "L0_attn_a_wk"),
                  ("wv", "L0_attn_a_wv"), ("wo", "L0_attn_a_wo"),
                  ("wqd", "L0_attn_b_Wq_down"), ("wkvd", "L0_attn_b_Wkv_down"),
                  ("wqu", "L0_attn_b_Wq_up"), ("wku", "L0_attn_b_Wk_up"),
                  ("wvu", "L0_attn_b_Wv_up"), ("wob", "L0_attn_b_Wo"),
                  ("wg", "L0_ffn_w_gate"), ("wu", "L0_ffn_w_up"),
                  ("wd", "L0_ffn_w_down")]:
        m[k] = pack_w(np.asarray(inp[nm]))
    for k, nm in [("bq", "L0_attn_a_bq"), ("bk", "L0_attn_a_bk"),
                  ("bv", "L0_attn_a_bv"), ("bo", "L0_attn_a_bo"),
                  ("bqd", "L0_attn_b_bq_down"), ("bkvd", "L0_attn_b_bkv_down"),
                  ("bqu", "L0_attn_b_bq_up"), ("bku", "L0_attn_b_bk_up"),
                  ("bvu", "L0_attn_b_bv_up"), ("bob", "L0_attn_b_bo")]:
        m[k] = pack_b(np.asarray(inp[nm]))
    return m


def _run_nc(nc, inp, n_cores=8, trace=False):
    m = _pack_inputs(inp)
    xf = m.pop("_x_full")
    in_maps = []
    for c in range(n_cores):
        d = dict(m)
        d["x"] = np.ascontiguousarray(
            xf[c * NSEQ:(c + 1) * NSEQ].reshape(NSEQ * D, S))
        in_maps.append(d)
    res = run_bass_kernel_spmd(nc, in_maps, core_ids=list(range(n_cores)),
                               trace=trace)
    y = np.concatenate([r["y"].reshape(NSEQ, D, S) for r in res.results], 0)
    return np.ascontiguousarray(y.transpose(0, 2, 1)), res


def _build_ffn_only():
    mk = MK()
    mk.setup()
    for seq in range(NSEQ):
        mk.ffn_call = seq
        mk.ffn(mk.t["x"], mk.t["y"], src_row0=seq * D, dst_row0=seq * D)
    return mk.finish()


def _build_full():
    mk = MK()
    t = mk.t
    mk.setup()
    mk.rope_tables()
    for seq in range(NSEQ):
        row = seq * D
        mk.attn_a(t["x"], t["xs0"], seq, src_row0=row, dst_row0=0)
        mk.ffn(t["xs0"], t["xs1"])
        mk.attn_b(t["xs1"], t["xs0"], seq)
        mk.ffn(t["xs0"], t["y"], dst_row0=row)
    return mk.finish()


def kernel(**inp):
    nc = _build_full()
    y, _ = _run_nc(nc, inp, n_cores=8)
    return y


# version 11
# speedup vs baseline: 1.0396x; 1.0396x over previous
import numpy as np
import concourse.bass as bass
import concourse.mybir as mybir
from concourse.alu_op_type import AluOpType
from concourse.bass_utils import run_bass_kernel_spmd
from contextlib import ExitStack

F32 = mybir.dt.float32
BF16 = mybir.dt.bfloat16
I32 = mybir.dt.int32
AF = mybir.ActivationFunctionType
AX = mybir.AxisListType.X

D = 2048
S = 2048
IC = D // 128
HID = 8192
HC = HID // 128
TT = 512
NT = S // TT
NSEQ = 2
NH = 16
LAT = 512
LC = LAT // 128
EPS = 1e-5
ISQ = float(1.0 / np.sqrt(128.0))


def pack_w(w):
    o, i = w.shape
    oc, ic = o // 128, i // 128
    return np.ascontiguousarray(
        w.reshape(oc, 128, ic, 128).transpose(0, 3, 2, 1).reshape(o, i))


def pack_b(b):
    return np.ascontiguousarray(b.reshape(-1, 128).T)


class Sem:
    __slots__ = ("h", "name", "n")

    def __init__(self, h, name):
        self.h, self.name, self.n = h, name, 0


class Prog:
    ENG = ("tensor", "vector", "scalar", "gpsimd", "sync")

    def __init__(self, nc, stack):
        self.nc = nc
        self.stack = stack
        self.ops = {e: [] for e in self.ENG}
        self.seen = {e: {} for e in self.ENG}
        self._sems = []

    def sem(self, name):
        s = Sem(self.stack.enter_context(self.nc.semaphore(name)), name)
        self._sems.append(s)
        return s

    def emit(self, eng, fn, wait=(), inc=None, inc_k=1):
        waits = []
        seen = self.seen[eng]
        for s, v in wait:
            if v <= 0 or seen.get(s.name, 0) >= v:
                continue
            seen[s.name] = v
            waits.append((s, v))
        if inc is not None:
            inc.n += inc_k
        self.ops[eng].append((fn, waits, inc, inc_k))

    def flush(self, name=""):
        ops = self.ops
        self.ops = {e: [] for e in self.ENG}
        active = [e for e in self.ENG if ops[e]]
        if not active:
            return
        with self.nc.Block() as blk:
            for e in active:
                def body(eng, _ops=ops[e]):
                    for fn, waits, inc, inc_k in _ops:
                        for s, v in waits:
                            eng.wait_ge(s.h, v)
                        r = fn(eng)
                        if inc is not None:
                            r.then_inc(inc.h, inc_k)
                getattr(blk, e)(body)
        for s in self._sems:
            if not s.name.startswith("d_"):
                for e in self.ENG:
                    self.seen[e][s.name] = s.n


class MK:
    def __init__(self):
        self.nc = bass.Bass("TRN2", target_bir_lowering=False, debug=False)
        nc = self.nc
        self.st = ExitStack()

        t = {}
        t["x"] = nc.dram_tensor("x", [NSEQ * D, S], F32, kind="ExternalInput").ap()
        t["y"] = nc.dram_tensor("y", [NSEQ * D, S], F32, kind="ExternalOutput").ap()
        for n, shp in [
            ("wq", [D, D]), ("wk", [D, D]), ("wv", [D, D]), ("wo", [D, D]),
            ("wqd", [LAT, D]), ("wkvd", [LAT, D]),
            ("wqu", [D // 2, LAT]), ("wku", [D // 2, LAT]),
            ("wvu", [D, LAT]), ("wob", [D, D]),
            ("wg", [HID, D]), ("wu", [HID, D]), ("wd", [D, HID]),
        ]:
            t[n] = nc.dram_tensor(n, shp, F32, kind="ExternalInput").ap()
        for n, noc in [("bq", IC), ("bk", IC), ("bv", IC), ("bo", IC),
                       ("bqd", LC), ("bkvd", LC), ("bqu", 8), ("bku", 8),
                       ("bvu", IC), ("bob", IC)]:
            t[n] = nc.dram_tensor(n, [128, noc], F32, kind="ExternalInput").ap()
        t["xs0"] = nc.dram_tensor("xs0", [D, S], F32).ap()
        t["xs1"] = nc.dram_tensor("xs1", [D, S], F32).ap()
        t["gated"] = nc.dram_tensor("gated", [128, NT * HC * TT], BF16).ap()
        t["aout"] = nc.dram_tensor("aout", [128, NT * IC * TT], BF16).ap()
        t["qlat_d"] = nc.dram_tensor("qlat_d", [128, LC * S], BF16).ap()
        t["kvlat_d"] = nc.dram_tensor("kvlat_d", [128, LC * S], BF16).ap()
        t["wo_bf"] = nc.dram_tensor("wo_bf", [D, D], BF16).ap()
        t["wob_bf"] = nc.dram_tensor("wob_bf", [D, D], BF16).ap()
        t["wg_bf"] = nc.dram_tensor("wg_bf", [HID, D], BF16).ap()
        t["wu_bf"] = nc.dram_tensor("wu_bf", [HID, D], BF16).ap()
        t["wd_bf"] = nc.dram_tensor("wd_bf", [D, HID], BF16).ap()
        self.t = t

        sbp = lambda n, shp, dt: self.st.enter_context(nc.sbuf_tensor(n, shp, dt))
        self.ln_x = None
        self.ones = sbp("ones", [128, 128], BF16)
        self.ident = sbp("ident", [128, 128], BF16)
        self.bias = {n: sbp("b_" + n, [128, {"bq": IC, "bk": IC, "bv": IC,
                                             "bo": IC, "bqd": LC, "bkvd": LC,
                                             "bqu": 8, "bku": 8, "bvu": IC,
                                             "bob": IC}[n]], F32)
                     for n in ("bq", "bk", "bv", "bo", "bqd", "bkvd",
                               "bqu", "bku", "bvu", "bob")}

        self.negb = {n: sbp("nb_" + n, [128, IC], F32) for n in ("bq", "bk")}
        self.COS = sbp("COS", [128, 2 * S], BF16)
        self.SIN = sbp("SIN", [128, 2 * S], BF16)
        self.cons = sbp("cons", [128, 8], F32)
        self.CONSTS = [EPS, 1e-8, 1e-30]

        self.p = Prog(nc, self.st)
        self.s = {n: self.p.sem(n) for n in
                  ("d_x0", "d_x1", "d_w0", "d_w1", "d_g0", "d_g1",
                   "d_s0", "d_s1", "d_pm", "d_q0", "d_q1", "d_q2", "d_q3",
                   "pe", "act", "dve", "pool")}
        self.dq = [self.s[f"d_q{i}"] for i in range(4)]
        self.dx = [self.s["d_x0"], self.s["d_x1"]]
        self.dw = [self.s["d_w0"], self.s["d_w1"]]
        self.dg = [self.s["d_g0"], self.s["d_g1"]]
        self.ds = [self.s["d_s0"], self.s["d_s1"]]
        self.dpm = self.s["d_pm"]
        self.ffn_call = 0
        self.prepc = {}
        self._uid = 0

    def sb(self, st, name, shp, dt):
        self._uid += 1
        return st.enter_context(
            self.nc.sbuf_tensor(f"{name}_{self._uid}", shp, dt))

    def pt(self, st, name, shp, dt):
        self._uid += 1
        return st.enter_context(
            self.nc.psum_tensor(f"{name}_{self._uid}", shp, dt))

    def ld(self, queue, sem, out, in_, wait=()):
        self.p.emit(queue, lambda e, o=out, i=in_: e.dma_start(out=o, in_=i),
                    wait=wait, inc=sem, inc_k=16)
        return sem.n

    def stq(self, sem, out, in_, wait=()):
        self.p.emit("sync", lambda e, o=out, i=in_: e.dma_start(out=o, in_=i),
                    wait=wait, inc=sem, inc_k=16)
        return sem.n

    def flush(self, name=""):
        dsems = [v for k, v in self.s.items() if k.startswith("d_")]
        self.p.emit("sync", lambda e: e.nop(),
                    wait=[(d, d.n) for d in dsems])
        self.p.flush(name)

    def setup(self):
        p, s, t = self.p, self.s, self.t
        p.emit("gpsimd", lambda e: e.memset(self.ones[:, :], 1.0),
               inc=s["pool"])
        p.emit("gpsimd", lambda e: e.affine_select(
            self.ident[:, :], self.ones[:, :], [[1, 128]],
            AluOpType.is_equal, 0.0, base=0, channel_multiplier=-1),
            wait=[(s["pool"], s["pool"].n)])
        for i, v in enumerate(self.CONSTS):
            ap = self.cons[:, i:i + 1]
            p.emit("gpsimd", lambda e, ap=ap, v=v: e.memset(ap, v))
            self.nc.const_aps.register(F32, float(v), ap)
        for n, b in self.bias.items():
            noc = b.shape[1]
            self.ld("sync", self.dpm, b[:, 0:noc], t[n][:, :])
        for n in ("bq", "bk"):
            p.emit("vector", lambda e, n=n: e.tensor_scalar_mul(
                self.negb[n][:, :], self.bias[n][:, :], -1.0),
                wait=[(self.dpm, self.dpm.n)])
        self.flush("setup")

    def ln(self, src, row0=0):
        nc, p, s = self.nc, self.p, self.s
        st = ExitStack()
        xf = [self.sb(st, f"ln_xf{i}", [128, IC * TT], F32)
              for i in range(2)]
        xb = [self.sb(st, "ln_xb", [128, IC * TT], BF16)] * 2
        xq = [self.sb(st, "ln_xq", [128, IC * TT], BF16)] * 2
        mpl = self.sb(st, "ln_m", [128, TT], F32)
        vpl = self.sb(st, "ln_v", [128, TT], F32)
        tpl = self.sb(st, "ln_t", [128, TT], F32)
        ntd = self.sb(st, "ln_ntd", [128, 2 * TT], F32)
        ntp = self.sb(st, "ln_ntp", [128, 2 * TT], F32)
        ps = [self.pt(st, f"ln_ps{i}", [128, 2 * TT], F32)
              for i in range(2)]
        pe, act, dve, pool = (s[k] for k in ("pe", "act", "dve", "pool"))
        dx = self.dx

        def tile(tt):
            return src[row0:row0 + D, tt * TT:(tt + 1) * TT].rearrange(
                "(ic p) t -> p ic t", p=128)

        def xfv(i):
            return xf[i][:, :].rearrange("p (ic t) -> p ic t", t=TT)

        ldc, sum_done = {}, {}
        cast_done, sq_done, stats_done, fin_done = {}, {}, {}, {}
        npool_done, ndve_done = {}, {}

        def fin_norm(tt):
            b = tt % 2
            pst = ps[b]
            p.emit("vector", lambda e, pst=pst: e.tensor_scalar_mul(
                mpl[:, :], pst[:, 0:TT], 1.0 / D),
                wait=[(pe, stats_done[tt]), (pool, npool_done.get(tt - 1, 0)),
                      (dve, ndve_done.get(tt - 1, 0))])
            p.emit("vector", lambda e, pst=pst: e.tensor_scalar_mul(
                vpl[:, :], pst[:, TT:2 * TT], 1.0 / D), inc=dve)
            mv_done = dve.n
            p.emit("vector", lambda e: e.tensor_mul(tpl[:, :], mpl[:, :], mpl[:, :]),
                   wait=[(dve, mv_done)], inc=dve)
            sq_mm = dve.n
            p.emit("vector", lambda e: e.tensor_sub(vpl[:, :], vpl[:, :], tpl[:, :]),
                   wait=[(dve, sq_mm)], inc=dve)
            var_done = dve.n
            p.emit("scalar", lambda e: e.activation(
                tpl[:, :], vpl[:, :], AF.Ln, bias=EPS),
                wait=[(dve, var_done)], inc=act)
            ln_done = act.n
            p.emit("scalar", lambda e: e.activation(
                tpl[:, :], tpl[:, :], AF.Exp, scale=-0.5),
                wait=[(act, ln_done)], inc=act)
            rstd_done = act.n
            fin_done[tt] = dve.n
            evens = [ic for ic in range(IC) if ic % 2 == 0]
            odds = [ic for ic in range(IC) if ic % 2 == 1]
            NW = 2
            for w0 in range(0, 8, NW):
                for k in range(NW):
                    ic = evens[w0 + k]
                    p.emit("vector", lambda e, b=b, ic=ic, k=k: e.tensor_sub(
                        ntd[:, k * TT:(k + 1) * TT],
                        xf[b][:, ic * TT:(ic + 1) * TT], mpl[:, :]),
                        wait=[(act, rstd_done), (dve, dve.n)] if k == 0 else [],
                        inc=dve if k == NW - 1 else None)
                sub_d = dve.n
                for k in range(NW):
                    ic = evens[w0 + k]
                    dst = self.ln_x[:, ic * S + tt * TT: ic * S + (tt + 1) * TT]
                    p.emit("vector", lambda e, dst=dst, k=k: e.tensor_mul(
                        dst, ntd[:, k * TT:(k + 1) * TT], tpl[:, :]),
                        wait=[(dve, sub_d)] if k == 0 else [],
                        inc=dve if k == NW - 1 else None)
            for w0 in range(0, 8, NW):
                for k in range(NW):
                    ic = odds[w0 + k]
                    p.emit("gpsimd", lambda e, b=b, ic=ic, k=k: e.tensor_sub(
                        ntp[:, k * TT:(k + 1) * TT],
                        xf[b][:, ic * TT:(ic + 1) * TT], mpl[:, :]),
                        wait=([(act, rstd_done), (pool, pool.n)]
                              if k == 0 else []),
                        inc=pool if k == NW - 1 else None)
                sub_p = pool.n
                for k in range(NW):
                    ic = odds[w0 + k]
                    dst = self.ln_x[:, ic * S + tt * TT: ic * S + (tt + 1) * TT]
                    p.emit("gpsimd", lambda e, dst=dst, k=k: e.tensor_mul(
                        dst, ntp[:, k * TT:(k + 1) * TT], tpl[:, :]),
                        wait=[(pool, sub_p)] if k == 0 else [],
                        inc=pool if k == NW - 1 else None)
            ndve_done[tt] = dve.n
            npool_done[tt] = pool.n

        ldc[0] = self.ld("sync", dx[0], xfv(0), tile(0))
        for tt in range(NT):
            b = tt % 2
            p.emit("vector", lambda e, b=b: e.tensor_copy(xb[b][:, :], xf[b][:, :]),
                   wait=[(dx[b], ldc[tt]), (pool, npool_done.get(tt - 2, 0)),
                         (pe, sum_done.get(tt - 1, 0))],
                   inc=dve)
            cast_done[tt] = dve.n
            p.emit("scalar", lambda e, b=b: e.activation(
                xq[b][:, 0:8 * TT], xf[b][:, 0:8 * TT], AF.Square),
                wait=[(dx[b], ldc[tt]), (pe, stats_done.get(tt - 1, 0))],
                inc=act)
            sq_done[tt] = act.n
            p.emit("gpsimd", lambda e, b=b: e.tensor_mul(
                xq[b][:, 8 * TT:], xf[b][:, 8 * TT:], xf[b][:, 8 * TT:]),
                wait=[(dx[b], ldc[tt]), (pe, stats_done.get(tt - 1, 0)),
                      (pool, npool_done.get(tt - 1, 0))],
                inc=pool)
            sqp_done = pool.n
            pst = ps[b]
            for ic in range(IC):
                p.emit("tensor", lambda e, b=b, ic=ic, pst=pst: e.matmul(
                    pst[:, 0:TT], self.ones[:, :],
                    xb[b][:, ic * TT:(ic + 1) * TT],
                    start=(ic == 0), stop=(ic == IC - 1)),
                    wait=([(dve, cast_done[tt]), (dve, fin_done.get(tt - 2, 0))]
                          if ic == 0 else []),
                    inc=pe if ic == IC - 1 else None)
            sum_done[tt] = pe.n
            for ic in range(IC):
                p.emit("tensor", lambda e, b=b, ic=ic, pst=pst: e.matmul(
                    pst[:, TT:2 * TT], self.ones[:, :],
                    xq[b][:, ic * TT:(ic + 1) * TT],
                    start=(ic == 0), stop=(ic == IC - 1)),
                    wait=([(act, sq_done[tt])] if ic == 0 else
                          ([(pool, sqp_done)] if ic == 8 else [])),
                    inc=pe if ic == IC - 1 else None)
            stats_done[tt] = pe.n
            if tt >= 1:
                fin_norm(tt - 1)
            if tt + 1 < NT:
                w = []
                if tt >= 1:
                    w = [(dve, ndve_done.get(tt - 1, 0)),
                         (act, sq_done[tt - 1]),
                         (pool, npool_done.get(tt - 1, 0))]
                ldc[tt + 1] = self.ld("sync", dx[(tt + 1) % 2],
                                      xfv((tt + 1) % 2), tile(tt + 1), wait=w)
        fin_norm(NT - 1)
        self.flush("ln")
        st.close()


    def ffn1(self, wgb=None, wub=None, pre=None):
        nc, p, s, t = self.nc, self.p, self.s, self.t
        call = self.ffn_call
        st = ExitStack()
        if wgb is None:
            wgb = [self.sb(st, f"f1wg{i}", [128, D], BF16)
                   for i in range(2)]
            wub = [self.sb(st, f"f1wu{i}", [128, D], BF16)
                   for i in range(2)]
        gbuf = self.sb(st, "f1g", [128, S], F32)
        ubuf = self.sb(st, "f1u", [128, S], F32)
        gout = [self.sb(st, f"f1o{i}", [128, S], BF16)
                for i in range(2)]
        psg = self.pt(st, "f1pg", [128, S], F32)
        psu = self.pt(st, "f1pu", [128, S], F32)
        pe, act, dve = (s[k] for k in ("pe", "act", "dve"))
        dw, ds = self.dw, self.ds

        use_prep = call > 0
        wg_src = t["wg_bf"] if use_prep else t["wg"]
        wu_src = t["wu_bf"] if use_prep else t["wu"]
        q = "gpsimd"
        wsem = self.dg

        wld, mm_pcg, evac_pcg, mult_done, st_done = {}, {}, {}, {}, {}
        prep_st = {}

        def load_h(h):
            w = [(pe, mm_pcg.get((h - 2, 7), 0))]
            if not use_prep and h >= 2:
                w.append((ds[h % 2], prep_st.get(h - 2, 0)))
            self.ld(q, wsem[h % 2], wgb[h % 2][:, :],
                    wg_src[h * 128:(h + 1) * 128, :], w)
            wld[h] = self.ld(q, wsem[h % 2], wub[h % 2][:, :],
                             wu_src[h * 128:(h + 1) * 128, :])

        if pre is None:
            load_h(0)
            load_h(1)
        else:
            wld[0], wld[1] = pre
        for h in range(HC):
            b = h % 2
            for j in range(8):
                proj, cg = divmod(j, 4)
                dst = psg if proj == 0 else psu
                wb = wgb if proj == 0 else wub
                for ic in range(IC):
                    w = []
                    if ic == 0:
                        w = [(wsem[h % 2], wld[h]),
                             (act, evac_pcg.get((h - 1, j), 0))]
                    p.emit("tensor", lambda e, b=b, cg=cg, ic=ic, dst=dst, wb=wb:
                           e.matmul(dst[:, cg * TT:(cg + 1) * TT],
                                    wb[b][:, ic * 128:(ic + 1) * 128],
                                    self.ln_x[:, ic * S + cg * TT:
                                              ic * S + cg * TT + TT],
                                    start=(ic == 0), stop=(ic == IC - 1)),
                           wait=w, inc=pe if ic == IC - 1 else None)
                mm_pcg[(h, j)] = pe.n
            for j in range(8):
                proj, cg = divmod(j, 4)
                src_ps = psg if proj == 0 else psu
                dst_sb = gbuf if proj == 0 else ubuf
                fn = AF.Silu if proj == 0 else AF.Copy
                p.emit("scalar", lambda e, cg=cg, src_ps=src_ps, dst_sb=dst_sb,
                       fn=fn: e.activation(
                           dst_sb[:, cg * TT:(cg + 1) * TT],
                           src_ps[:, cg * TT:(cg + 1) * TT], fn),
                       wait=[(pe, mm_pcg[(h, j)]),
                             (dve, mult_done.get(h - 1, 0) if j == 0 else 0)],
                       inc=act)
                evac_pcg[(h, j)] = act.n
            gb = gout[b]
            p.emit("vector", lambda e, gb=gb: e.tensor_mul(
                gb[:, :], gbuf[:, :], ubuf[:, :]),
                wait=[(act, evac_pcg[(h, 7)]), (ds[h % 2], st_done.get(h - 2, 0))],
                inc=dve)
            mult_done[h] = dve.n
            gdst = t["gated"].rearrange("p (q h t) -> h p q t", h=HC, t=TT)[h]
            st_done[h] = self.stq(ds[h % 2], gdst,
                                  gb[:, :].rearrange("p (q t) -> p q t", t=TT),
                                  [(dve, mult_done[h])])
            if not use_prep:
                self.stq(ds[h % 2], t["wg_bf"][h * 128:(h + 1) * 128, :],
                         wgb[b][:, :], [(wsem[h % 2], wld[h])])
                prep_st[h] = self.stq(ds[h % 2],
                                      t["wu_bf"][h * 128:(h + 1) * 128, :],
                                      wub[b][:, :], [])
                st_done[h] = prep_st[h]
            if h + 2 < HC:
                load_h(h + 2)
        self.flush(f"ffn1_{call}")
        st.close()

    def proj_scratch(self, name, rhs_dram, ncc, wsrc, wbf, mode, dst, src,
                     bias_ap=None, src_row0=0, dst_row0=0):
        nc, p, s, t = self.nc, self.p, self.s, self.t
        st = ExitStack()
        gres = self.sb(st, "pj_g", [128, ncc * TT], BF16)
        NBUF = 3 if (ncc <= 16 and mode == "use") else 2
        wdb = [self.sb(st, f"pj_w{i}", [128, 2 * ncc * 128], BF16)
               for i in range(NBUF)]
        xres = self.sb(st, "pj_x", [128, IC * TT], F32)
        obuf = [self.sb(st, f"pj_o{i}", [128, 2 * TT], F32)
                for i in range(2)]
        pso = self.pt(st, "pj_ps", [128, 4 * TT], F32)
        pe, dve = s["pe"], s["dve"]
        dx, dw, ds = self.dx, self.dw, self.ds
        NE = 8
        WB = ncc * 128
        NQ = 4
        NL = NT * NE
        mm_done, ep_done, ost_done = {}, {}, {}
        wld_map, pst_done = {}, {}

        def emit_wload(l):
            ttp, e8p = divmod(l, NE)
            bb = l % NBUF
            sb2 = e8p % 2
            use_bf = (mode == "use") or (mode == "first" and ttp > 0)
            ws = wbf if use_bf else wsrc
            q = "sync" if use_bf else "gpsimd"
            lsem = dw[sb2] if use_bf else self.dg[sb2]
            wpre = [(pe, mm_done.get(l - 2, 0)),
                    (ds[sb2], ost_done.get(l - 2, 0))]
            if mode == "first" and ttp == 1 and e8p == 0:
                wpre += [(ds[0], ds[0].n), (ds[1], ds[1].n)]
            for oo in range(2):
                wld_map[l] = (lsem, self.ld(
                    q, lsem, wdb[bb][:, oo * WB:(oo + 1) * WB],
                    ws[(e8p * 2 + oo) * 128:(e8p * 2 + oo + 1) * 128, :],
                    wpre if oo == 0 else ()))
            if mode == "first" and ttp == 0:
                for oo in range(2):
                    pst_done[l] = self.stq(
                        ds[sb2],
                        wbf[(e8p * 2 + oo) * 128:
                            (e8p * 2 + oo + 1) * 128, :],
                        wdb[bb][:, oo * WB:(oo + 1) * WB],
                        [wld_map[l]])

        for li in range(NBUF - 1):
            emit_wload(li)
        for tt in range(NT):
            w_gres = [(pe, mm_done.get(tt * NE - 1, 0))]
            w_xres = [(dve, ep_done.get(tt * NE - 1, 0))]
            g_ld = {}
            qsz = ncc * TT // NQ
            for qq in range(NQ):
                g_ld[qq] = (self.dq[qq], self.ld(
                    "sync", self.dq[qq], gres[:, qq * qsz:(qq + 1) * qsz],
                    rhs_dram[:, tt * ncc * TT + qq * qsz:
                             tt * ncc * TT + (qq + 1) * qsz],
                    w_gres if qq == 0 else ()))
            x_ld = self.ld("sync", dx[tt % 2],
                           xres[:, :].rearrange("p (ic t) -> p ic t", t=TT),
                           src[src_row0:src_row0 + D,
                               tt * TT:(tt + 1) * TT].rearrange(
                               "(ic p) t -> p ic t", p=128), w_xres)
            for e8 in range(NE):
                l = tt * NE + e8
                b = e8 % 2
                wb_ = l % NBUF
                pb = b * 2
                hq = ncc // NQ
                for oo in range(2):
                    for h in range(ncc):
                        w = []
                        if h == 0 and oo == 0:
                            w = [wld_map[l], g_ld[0],
                                 (dve, ep_done.get(l - 2, 0))]
                        elif h % hq == 0 and e8 == 0 and oo == 0:
                            w = [g_ld[h // hq]]
                        p.emit("tensor", lambda e, wb_=wb_, h=h, oo=oo, pb=pb:
                               e.matmul(pso[:, (pb + oo) * TT:(pb + oo + 1) * TT],
                                        wdb[wb_][:, oo * WB + h * 128:
                                                 oo * WB + (h + 1) * 128],
                                        gres[:, h * TT:(h + 1) * TT],
                                        start=(h == 0), stop=(h == ncc - 1)),
                               wait=w,
                               inc=pe if (h == ncc - 1 and oo == 1) else None)
                mm_done[l] = pe.n
                if l + NBUF - 1 < NL:
                    emit_wload(l + NBUF - 1)
                ob = obuf[b]
                for oo in range(2):
                    oc = e8 * 2 + oo
                    w = ([(pe, mm_done[l]), (ds[b], ost_done.get(l - 2, 0)),
                          (dx[tt % 2], x_ld)]
                         if oo == 0 else [])
                    if bias_ap is None:
                        p.emit("vector", lambda e, ob=ob, oo=oo, oc=oc, pb=pb:
                               e.tensor_add(
                                   ob[:, oo * TT:(oo + 1) * TT],
                                   pso[:, (pb + oo) * TT:(pb + oo + 1) * TT],
                                   xres[:, oc * TT:(oc + 1) * TT]),
                               wait=w, inc=dve if oo == 1 else None)
                    else:
                        p.emit("vector", lambda e, ob=ob, oo=oo, oc=oc, pb=pb:
                               e.scalar_tensor_tensor(
                                   ob[:, oo * TT:(oo + 1) * TT],
                                   pso[:, (pb + oo) * TT:(pb + oo + 1) * TT],
                                   bias_ap[:, oc:oc + 1],
                                   xres[:, oc * TT:(oc + 1) * TT],
                                   AluOpType.add, AluOpType.add),
                               wait=w, inc=dve if oo == 1 else None)
                ep_done[l] = dve.n
                for oo in range(2):
                    oc = e8 * 2 + oo
                    ost_done[l] = self.stq(
                        ds[b],
                        dst[dst_row0 + oc * 128: dst_row0 + (oc + 1) * 128,
                            tt * TT:(tt + 1) * TT],
                        ob[:, oo * TT:(oo + 1) * TT],
                        [(dve, ep_done[l])] if oo == 0 else ())
        self.flush(f"pj_{name}_{self._uid}")
        st.close()


    def rope_tables(self):
        nc, p, s = self.nc, self.p, self.s
        import math
        st = ExitStack()
        i2 = self.sb(st, "rp_i2", [128, 2 * S], I32)
        trow = self.sb(st, "rp_t", [128, S], F32)
        ksm = self.sb(st, "rp_ks", [128, 2], I32)
        ivf = self.sb(st, "rp_v", [128, 2], F32)
        ivfT = [self.sb(st, f"rp_vt{j}", [1, 128], F32) for j in range(2)]
        identf = self.sb(st, "rp_idf", [128, 128], F32)
        fr = self.sb(st, "rp_f", [128, 2 * S], F32)
        kR = self.sb(st, "rp_r", [128, 2 * S], F32)
        rr = self.sb(st, "rp_rr", [128, 2 * S], F32)
        PS = self.pt(st, "rp_ps", [128, 2 * TT], F32)
        PT = self.pt(st, "rp_pt", [128, 128], F32)
        pe, act, dve, pool = s["pe"], s["act"], s["dve"], s["pool"]
        TAU = 2.0 * math.pi
        C1 = 12868.0 / 2048.0
        C2 = float(np.float32(TAU - C1))
        C3 = float(TAU - C1 - C2)

        p.emit("gpsimd", lambda e: e.iota(i2[0:1, 0:S], [[1, S]],
                                          channel_multiplier=0))
        p.emit("gpsimd", lambda e: e.iota(ksm[:, :], [[128, 2]],
                                          channel_multiplier=1))
        p.emit("gpsimd", lambda e: e.tensor_copy(identf[:, :],
                                                 self.ident[:, :]),
               inc=pool)
        ic0 = pool.n
        p.emit("vector", lambda e: e.tensor_copy(trow[0:1, :], i2[0:1, 0:S]),
               wait=[(pool, ic0)])
        p.emit("vector", lambda e: e.tensor_copy(ivf[:, :], ksm[:, :]),
               inc=dve)
        cp0 = dve.n
        p.emit("scalar", lambda e: e.activation(
            ivf[:, :], ivf[:, :], AF.Exp, scale=-math.log(10000.0) / 256.0),
            wait=[(dve, cp0)], inc=act)
        iv0 = act.n
        tpc = {}
        for j in range(2):
            p.emit("tensor", lambda e, j=j: e.transpose(
                PT[0:1, 0:128], ivf[:, j:j + 1], identf[:, :]),
                wait=[(act, iv0), (dve, tpc.get(j - 1, 0))], inc=pe)
            tp = pe.n
            p.emit("vector", lambda e, j=j: e.tensor_copy(
                ivfT[j][0:1, :], PT[0:1, 0:128]),
                wait=[(pe, tp)], inc=dve)
            tpc[j] = dve.n
        ivt = dve.n
        mmc, cpc = {}, {}
        for j in range(2):
            for cg in range(NT):
                i = j * NT + cg
                pb = (i % 2) * TT
                p.emit("tensor", lambda e, j=j, cg=cg, pb=pb: e.matmul(
                    PS[:, pb:pb + TT], ivfT[j][0:1, :],
                    trow[0:1, cg * TT:(cg + 1) * TT],
                    start=True, stop=True),
                    wait=[(dve, ivt if i < 2 else cpc.get(i - 2, 0))],
                    inc=pe)
                mmc[i] = pe.n
                dst = fr[:, j * S + cg * TT: j * S + (cg + 1) * TT]
                p.emit("vector", lambda e, dst=dst, pb=pb: e.tensor_copy(
                    dst, PS[:, pb:pb + TT]),
                    wait=[(pe, mmc[i])], inc=dve)
                cpc[i] = dve.n
        for phase, dstT in ((0.0, self.SIN), (0.5 * math.pi, self.COS)):
            p.emit("vector", lambda e, phase=phase: e.tensor_scalar(
                kR[:, :], fr[:, :], 1.0 / TAU, phase / TAU,
                AluOpType.mult, AluOpType.add),
                wait=[(dve, dve.n), (act, act.n)], inc=dve)
            p.emit("vector", lambda e: e.tensor_copy(i2[:, :], kR[:, :]),
                   wait=[(dve, dve.n)], inc=dve)
            p.emit("vector", lambda e: e.tensor_copy(kR[:, :], i2[:, :]),
                   wait=[(dve, dve.n)], inc=dve)
            p.emit("vector", lambda e: e.scalar_tensor_tensor(
                rr[:, :], kR[:, :], -C1, fr[:, :],
                AluOpType.mult, AluOpType.add), wait=[(dve, dve.n)], inc=dve)
            p.emit("vector", lambda e: e.scalar_tensor_tensor(
                rr[:, :], kR[:, :], -C2, rr[:, :],
                AluOpType.mult, AluOpType.add), wait=[(dve, dve.n)], inc=dve)
            p.emit("vector", lambda e: e.scalar_tensor_tensor(
                rr[:, :], kR[:, :], -C3, rr[:, :],
                AluOpType.mult, AluOpType.add), wait=[(dve, dve.n)], inc=dve)
            if phase != 0.0:
                p.emit("vector", lambda e, phase=phase: e.tensor_scalar_add(
                    rr[:, :], rr[:, :], phase), wait=[(dve, dve.n)], inc=dve)
            p.emit("vector", lambda e: e.tensor_scalar(
                kR[:, :], rr[:, :], math.pi, None, AluOpType.is_gt),
                wait=[(dve, dve.n)], inc=dve)
            p.emit("vector", lambda e: e.scalar_tensor_tensor(
                rr[:, :], kR[:, :], -TAU, rr[:, :],
                AluOpType.mult, AluOpType.add), wait=[(dve, dve.n)], inc=dve)
            p.emit("vector", lambda e: e.tensor_scalar(
                kR[:, :], rr[:, :], -math.pi, None, AluOpType.is_lt),
                wait=[(dve, dve.n)], inc=dve)
            p.emit("vector", lambda e: e.scalar_tensor_tensor(
                rr[:, :], kR[:, :], TAU, rr[:, :],
                AluOpType.mult, AluOpType.add), wait=[(dve, dve.n)], inc=dve)
            p.emit("vector", lambda e: e.tensor_scalar_min(
                rr[:, :], rr[:, :], math.pi), wait=[(dve, dve.n)], inc=dve)
            p.emit("vector", lambda e: e.tensor_scalar_max(
                rr[:, :], rr[:, :], -math.pi), wait=[(dve, dve.n)], inc=dve)
            rdone = dve.n
            p.emit("scalar", lambda e, dstT=dstT: e.activation(
                dstT[:, :], rr[:, :], AF.Sin), wait=[(dve, rdone)], inc=act)
        self.flush("rope")
        st.close()

    def attn_b_lats(self):
        nc, p, s, t = self.nc, self.p, self.s, self.t
        st = ExitStack()
        wl = self.sb(st, "ab_wl", [128, 2 * LC * D], BF16)
        obuf = [self.sb(st, f"ab_lo{i}", [128, TT], BF16) for i in range(2)]
        PA = self.pt(st, "ab_pa", [128, 2048], F32)
        pe, dve = s["pe"], s["dve"]
        dg, ds = self.dg, self.ds
        ld0 = self.ld("gpsimd", dg[0], wl[:, 0:LC * D].rearrange(
            "p (oc m) -> p oc m", m=D), t["wqd"][:, :].rearrange(
            "(oc q) m -> q oc m", q=128))
        ld0 = self.ld("gpsimd", dg[0], wl[:, LC * D:].rearrange(
            "p (oc m) -> p oc m", m=D), t["wkvd"][:, :].rearrange(
            "(oc q) m -> q oc m", q=128))
        ev, stc = {}, {}
        for pj, (dst_d, bn) in enumerate(
                [(t["qlat_d"], "bqd"), (t["kvlat_d"], "bkvd")]):
            bp = self.bias[bn]
            for oc in range(LC):
                for cg in range(NT):
                    u = pj * 16 + oc * NT + cg
                    bk = u % 4
                    ub = u % 2
                    for ic in range(IC):
                        w = []
                        if ic == 0:
                            w = [(dg[0], ld0), (dve, ev.get(u - 4, 0))]
                        p.emit("tensor", lambda e, pj=pj, oc=oc, cg=cg, ic=ic,
                               bk=bk: e.matmul(
                                   PA[:, bk * TT:(bk + 1) * TT],
                                   wl[:, pj * LC * D + oc * D + ic * 128:
                                      pj * LC * D + oc * D + (ic + 1) * 128],
                                   self.ln_x[:, ic * S + cg * TT:
                                             ic * S + cg * TT + TT],
                                   start=(ic == 0), stop=(ic == IC - 1)),
                               wait=w, inc=pe if ic == IC - 1 else None)
                    mm = pe.n
                    p.emit("vector", lambda e, ub=ub, bk=bk, oc=oc, bp=bp:
                           e.tensor_scalar_add(
                               obuf[ub][:, :], PA[:, bk * TT:(bk + 1) * TT],
                               bp[:, oc:oc + 1]),
                           wait=[(pe, mm), (ds[ub], stc.get(u - 2, 0))],
                           inc=dve)
                    ev[u] = dve.n
                    stc[u] = self.stq(
                        ds[ub],
                        dst_d[:, (oc * NT + cg) * TT:(oc * NT + cg + 1) * TT],
                        obuf[ub][:, :], [(dve, ev[u])])
        self.flush("ab_lats")
        st.close()

    def attn_b_ups(self, qhalf, khalf, kvlat):
        nc, p, s, t = self.nc, self.p, self.s, self.t
        st = ExitStack()
        wu2 = self.sb(st, "ab_wu", [128, 2 * 8 * LAT], BF16)
        qlat = self.sb(st, "ab_ql", [128, LC * S], BF16)
        PA = self.pt(st, "ab_pa2", [128, 2048], F32)
        pe, dve = s["pe"], s["dve"]
        dg, dx = self.dg, self.dx
        ld0 = self.ld("gpsimd", dg[0], wu2[:, 0:8 * LAT].rearrange(
            "p (oc m) -> p oc m", m=LAT), t["wqu"][:, :].rearrange(
            "(oc q) m -> q oc m", q=128))
        ld0 = self.ld("gpsimd", dg[0], wu2[:, 8 * LAT:].rearrange(
            "p (oc m) -> p oc m", m=LAT), t["wku"][:, :].rearrange(
            "(oc q) m -> q oc m", q=128))
        lx = self.ld("sync", dx[0], qlat[:, :], t["qlat_d"][:, :])
        lx = self.ld("sync", dx[0], kvlat[:, :], t["kvlat_d"][:, :])
        ev = {}
        for pj, (src_t, dst_t, bn) in enumerate(
                [(qlat, qhalf, "bqu"), (kvlat, khalf, "bku")]):
            bp = self.bias[bn]
            for oc in range(8):
                for cg in range(NT):
                    u = pj * 32 + oc * NT + cg
                    bk = u % 4
                    for lc in range(LC):
                        w = []
                        if lc == 0:
                            w = [(dg[0], ld0), (dx[0], lx),
                                 (dve, ev.get(u - 4, 0))]
                        p.emit("tensor", lambda e, pj=pj, oc=oc, cg=cg, lc=lc,
                               bk=bk, src_t=src_t: e.matmul(
                                   PA[:, bk * TT:(bk + 1) * TT],
                                   wu2[:, pj * 8 * LAT + oc * LAT + lc * 128:
                                       pj * 8 * LAT + oc * LAT + (lc + 1) * 128],
                                   src_t[:, lc * S + cg * TT:
                                         lc * S + cg * TT + TT],
                                   start=(lc == 0), stop=(lc == LC - 1)),
                               wait=w, inc=pe if lc == LC - 1 else None)
                    mm = pe.n
                    p.emit("vector", lambda e, dst_t=dst_t, oc=oc, cg=cg,
                           bk=bk, bp=bp: e.tensor_scalar_add(
                               dst_t[:, oc * S + cg * TT:
                                     oc * S + (cg + 1) * TT],
                               PA[:, bk * TT:(bk + 1) * TT],
                               bp[:, oc:oc + 1]),
                           wait=[(pe, mm)], inc=dve)
                    ev[u] = dve.n
        self.flush("ab_ups")
        st.close()

    def attn_b_heads(self, qhalf, khalf, kvlat):
        nc, p, s, t = self.nc, self.p, self.s, self.t
        st = ExitStack()
        wvb = [self.sb(st, f"ab_wv{i}", [128, LAT], BF16) for i in range(2)]
        qrh = self.sb(st, "ab_qr", [128, S], BF16)
        krh = self.sb(st, "ab_kr", [128, S], BF16)
        tmx = self.sb(st, "ab_tm", [128, S], BF16)
        vfm = self.sb(st, "ab_vf", [128, S], BF16)
        vtm = self.sb(st, "ab_vt", [128, S], BF16)
        sqq = self.sb(st, "ab_sq", [128, S], BF16)
        nrm = self.sb(st, "ab_nr", [128, 4], F32)
        prT = self.sb(st, "ab_pr", [128, IC * TT], BF16)
        rcp = self.sb(st, "ab_rc", [128, TT], BF16)
        ob = [self.sb(st, f"ab_o{i}", [128, S], BF16) for i in range(2)]
        PA = self.pt(st, "ab_pa3", [128, 2048], F32)
        PB = self.pt(st, "ab_pb", [128, 1024], F32)
        PAV = self.pt(st, "ab_pav", [128, 512], F32)
        PT = [self.pt(st, "ab_pt", [128, 1024], BF16)] * 2
        pe, act, dve = s["pe"], s["act"], s["dve"]
        dg, ds = self.dg, self.ds

        wld, vdone, ostore, mdone = {}, {}, {}, {}
        exdone = {}
        lastmul = {}
        av_prev2 = 0

        def load_v(h):
            b = h % 2
            wld[h] = self.ld("gpsimd", dg[b], wvb[b][:, :],
                             t["wvu"][h * 128:(h + 1) * 128, :],
                             [(pe, vdone.get(h - 2, 0))])

        load_v(0)
        load_v(1)
        DBG_SKIPV = getattr(self, "dbg_skipv", False)
        for h in range(NH):
            b = h % 2
            if getattr(self, "dbg_serpre", False) and h > 0:
                p.emit("tensor", lambda e: e.nop(),
                       wait=[(act, act.n), (dve, dve.n), (s["pool"], s["pool"].n)])
                p.emit("vector", lambda e: e.nop(),
                       wait=[(pe, pe.n), (act, act.n)])
                p.emit("scalar", lambda e: e.nop(),
                       wait=[(pe, pe.n), (dve, dve.n)])
            if h < 8:
                q_h = qhalf[:, h * S:(h + 1) * S]
                k_h = khalf[:, h * S:(h + 1) * S]
                rope_done = 0
            else:
                j = h - 8
                jj = j if j < 4 else j - 4
                c = jj % 2
                x1q = qhalf[:, jj * S:(jj + 1) * S]
                x2q = qhalf[:, (jj + 4) * S:(jj + 5) * S]
                x1k = khalf[:, jj * S:(jj + 1) * S]
                x2k = khalf[:, (jj + 4) * S:(jj + 5) * S]
                CS = self.COS[:, c * S:(c + 1) * S]
                SN = self.SIN[:, c * S:(c + 1) * S]
                if j < 4:
                    combos = [(x1q, CS, x2q, SN, AluOpType.subtract, qrh),
                              (x1k, CS, x2k, SN, AluOpType.subtract, krh)]
                else:
                    combos = [(x1q, SN, x2q, CS, AluOpType.add, qrh),
                              (x1k, SN, x2k, CS, AluOpType.add, krh)]
                for a1, c1, a2, c2, op, dstr in combos:
                    p.emit("vector", lambda e, a1=a1, c1=c1: e.tensor_mul(
                        tmx[:, :], a1, c1),
                        wait=[(dve, dve.n),
                              (pe, mdone.get(h - 1, 0))], inc=dve)
                    t1 = dve.n
                    p.emit("vector", lambda e, a2=a2, c2=c2, dstr=dstr:
                           e.tensor_mul(dstr[:, :], a2, c2),
                           wait=[(dve, t1)], inc=dve)
                    t2 = dve.n
                    p.emit("vector", lambda e, op=op, dstr=dstr:
                           e.tensor_tensor(dstr[:, :], tmx[:, :], dstr[:, :],
                                           op),
                           wait=[(dve, t2)], inc=dve)
                q_h, k_h = qrh, krh
                rope_done = dve.n
            nq_done = {}
            for i, zz in enumerate((q_h, k_h)):
                p.emit("scalar", lambda e, zz=zz: e.activation(
                    sqq[:, :], zz[:, :], AF.Square),
                    wait=[(dve, rope_done),
                          (pe, nq_done.get(i - 1, 0)),
                          (act, self.ab_rcd.get(h - 1, 0))], inc=act)
                sq_d = act.n
                for cg in range(NT):
                    w = [(act, sq_d), (dve, self.ab_nbg)]
                    if cg == 0 and i == 0:
                        w.append((dve, self.ab_nrm.get(h - 1, 0)))
                    p.emit("tensor", lambda e, cg=cg: e.matmul(
                        PB[:, 0:TT], self.ones[:, :],
                        sqq[:, cg * TT:(cg + 1) * TT],
                        start=True, stop=True),
                        wait=w, inc=pe)
                    nmm = pe.n
                    p.emit("vector", lambda e, cg=cg, i=i: e.reduce_max(
                        nrm[:, 3:4], PB[:, 0:TT], axis=AX),
                        wait=([(pe, nmm), (dve, dve.n),
                               (act, self.ab_rcd.get(h - 1, 0))]
                              if (i == 0 and cg == 0) else
                              [(pe, nmm), (dve, dve.n)]), inc=dve)
                    rmx = dve.n
                    if cg == 0:
                        p.emit("vector", lambda e, i=i: e.tensor_copy(
                            nrm[:, i:i + 1], nrm[:, 3:4]),
                            wait=[(dve, rmx)], inc=dve)
                    else:
                        p.emit("vector", lambda e, i=i: e.tensor_max(
                            nrm[:, i:i + 1], nrm[:, i:i + 1], nrm[:, 3:4]),
                            wait=[(dve, rmx)], inc=dve)
                    self.ab_nbg = dve.n
                nq_done[i] = pe.n
            p.emit("vector", lambda e: e.tensor_mul(
                nrm[:, 2:3], nrm[:, 0:1], nrm[:, 1:2]),
                wait=[(dve, dve.n)], inc=dve)
            nprod = dve.n
            p.emit("scalar", lambda e: e.activation(
                nrm[:, 2:3], nrm[:, 2:3], AF.Ln, bias=1e-30),
                wait=[(dve, nprod)], inc=act)
            lnn = act.n
            p.emit("scalar", lambda e: e.activation(
                nrm[:, 2:3], nrm[:, 2:3], AF.Exp, scale=0.5),
                wait=[(act, lnn)], inc=act)
            mdn = act.n
            p.emit("vector", lambda e: e.tensor_scalar_mul(
                nrm[:, 3:4], nrm[:, 2:3], -ISQ),
                wait=[(act, mdn)], inc=dve)
            bias_done = dve.n
            if DBG_SKIPV and h > 0:
                pass
            else:
             for cg in range(NT):
                for lc in range(LC):
                    w = []
                    if lc == 0 and cg == 0:
                        w = [(dg[b], wld[h]), (dve, lastmul.get(h - 1, 0))]
                    p.emit("tensor", lambda e, b=b, cg=cg, lc=lc: e.matmul(
                        PA[:, cg * TT:(cg + 1) * TT],
                        wvb[b][:, lc * 128:(lc + 1) * 128],
                        kvlat[:, lc * S + cg * TT: lc * S + cg * TT + TT],
                        start=(lc == 0), stop=(lc == LC - 1)),
                        wait=w, inc=pe if lc == LC - 1 else None)
                mmv = pe.n
                p.emit("vector", lambda e, cg=cg, h=h: e.tensor_scalar_add(
                    vfm[:, cg * TT:(cg + 1) * TT],
                    PA[:, cg * TT:(cg + 1) * TT],
                    self.bias["bvu"][:, h:h + 1]),
                    wait=[(pe, mmv)], inc=dve if cg == NT - 1 else None)
            vev = dve.n
            for g in range(2):
                pt = PT[g]
                for c8 in range(8):
                    c = g * 8 + c8
                    w = []
                    if c8 == 0:
                        prevu = (self.ab_te.get((h, 0), 0) if g == 1
                                 else self.ab_te.get((h - 1, 1), 0))
                        w = [(dve, vev), (dve, prevu)]
                    p.emit("tensor", lambda e, pt=pt, c8=c8, c=c: e.transpose(
                        pt[:, c8 * 128:(c8 + 1) * 128],
                        vfm[:, c * 128:(c + 1) * 128], self.ident[:, :]),
                        wait=w, inc=pe if c8 == 7 else None)
                tp = pe.n
                for c8 in range(8):
                    c = g * 8 + c8
                    p.emit("vector", lambda e, pt=pt, c8=c8, c=c:
                           e.tensor_copy(vtm[:, c * 128:(c + 1) * 128],
                                         pt[:, c8 * 128:(c8 + 1) * 128]),
                           wait=[(pe, tp)] if c8 == 0 else [],
                           inc=dve if c8 == 7 else None)
                self.ab_te[(h, g)] = dve.n
            vdone[h] = pe.n
            vtm_done = dve.n
            mul_prev = self.ab_mul.get(h - 1, 0)
            DBG_ST4 = 0 if (getattr(self, "dbg_skips", False) and h > 0) else NT
            for st4 in range(DBG_ST4):
                dpb = (st4 % 2) * TT
                exd = {}
                for skc in range(IC):
                    pb = (skc % 4) * TT
                    w = [(dve, bias_done)]
                    if skc >= 4:
                        w.append((act, exd[(skc - 4) // 2]))
                    elif st4 > 0:
                        w.append((act, exdone.get((h, st4 - 1), 0)))
                    else:
                        w.append((act, exdone.get((h - 1, NT - 1), 0)))
                    p.emit("tensor", lambda e, skc=skc, st4=st4, pb=pb,
                           k_h=k_h, q_h=q_h: e.matmul(
                               PA[:, pb:pb + TT],
                               k_h[:, skc * 128:(skc + 1) * 128],
                               q_h[:, st4 * TT:(st4 + 1) * TT],
                               start=True, stop=True),
                           wait=w, inc=pe)
                    smm = pe.n
                    if skc % 2 == 1:
                        j = (skc - 1) // 2
                        pb0 = ((skc - 1) % 4) * TT
                        p.emit("scalar", lambda e, skc=skc, pb0=pb0:
                               e.activation(
                                   prT[:, (skc - 1) * TT:(skc + 1) * TT],
                                   PA[:, pb0:pb0 + 2 * TT],
                                   AF.Exp, bias=nrm[:, 3:4], scale=ISQ),
                               wait=([(pe, smm), (pe, av_prev2)] if skc == 1
                                     else [(pe, smm)]), inc=act)
                        exd[j] = act.n
                        if j >= 2:
                            for kk in (skc - 5, skc - 4):
                                p.emit("tensor", lambda e, kk=kk, st4=st4,
                                       dpb=dpb: e.matmul(
                                           PB[:, dpb:dpb + TT],
                                           self.ones[:, :],
                                           prT[:, kk * TT:(kk + 1) * TT],
                                           start=(kk == 0), stop=False),
                                       wait=[(act, exd[j - 2]),
                                             (act, self.ab_lnd
                                              if kk == 0 else 0),
                                             (dve, mul_prev
                                              if kk == 0 else 0)])
                                p.emit("tensor", lambda e, kk=kk, st4=st4:
                                       e.matmul(PAV[:, 0:TT],
                                                vtm[:, kk * 128:(kk + 1) * 128],
                                                prT[:, kk * TT:(kk + 1) * TT],
                                                start=(kk == 0), stop=False),
                                       wait=[(dve, vtm_done
                                              if kk == 0 else 0),
                                             (dve, mul_prev
                                              if kk == 0 else 0)])
                exdone[(h, st4)] = act.n
                for kk in (IC - 4, IC - 3, IC - 2, IC - 1):
                    p.emit("tensor", lambda e, kk=kk, st4=st4, dpb=dpb:
                           e.matmul(PB[:, dpb:dpb + TT], self.ones[:, :],
                                    prT[:, kk * TT:(kk + 1) * TT],
                                    start=False, stop=(kk == IC - 1)),
                           wait=[(act, exd[kk // 2])] if kk % 2 == 0 else [],
                           inc=pe if kk == IC - 1 else None)
                    p.emit("tensor", lambda e, kk=kk, st4=st4:
                           e.matmul(PAV[:, 0:TT],
                                    vtm[:, kk * 128:(kk + 1) * 128],
                                    prT[:, kk * TT:(kk + 1) * TT],
                                    start=False, stop=(kk == IC - 1)),
                           inc=pe if kk == IC - 1 else None)
                avm = pe.n
                av_prev2 = avm
                self.ab_dn[(h, st4)] = pe.n
                p.emit("scalar", lambda e, st4=st4, dpb=dpb: e.activation(
                    sqq[:, 0:TT], PB[:, dpb:dpb + TT], AF.Ln, bias=1e-30),
                    wait=[(pe, avm), (dve, dve.n)], inc=act)
                lnd = act.n
                self.ab_lnd = lnd
                p.emit("scalar", lambda e: e.activation(
                    rcp[:, :], sqq[:, 0:TT], AF.Exp, scale=-1.0),
                    wait=[(act, lnd), (dve, mul_prev)], inc=act)
                rcd = act.n
                p.emit("vector", lambda e, st4=st4, ob=ob[b]: e.tensor_mul(
                    ob[:, st4 * TT:(st4 + 1) * TT], PAV[:, 0:TT],
                    rcp[:, :]),
                    wait=[(pe, avm), (act, rcd),
                          (ds[b], ostore.get(h - 2, 0))], inc=dve)
                mul_prev = dve.n
                lastmul[h] = dve.n
            self.ab_mul[h] = lastmul.get(h, mul_prev)
            self.ab_rcd[h] = act.n
            self.ab_nrm[h] = lastmul.get(h, mul_prev)
            mdone[h] = pe.n

            adst = t["aout"].rearrange("p (q c t) -> c p q t", c=IC, t=TT)[h]
            ostore[h] = self.stq(
                ds[b], adst, ob[b][:, :].rearrange("p (q t) -> p q t", t=TT),
                [(dve, lastmul.get(h, 0))])
            if h + 2 < NH:
                load_v(h + 2)
            if getattr(self, "dbg_headbar", False):
                self.flush(f"abh_{h}")
            if getattr(self, "dbg_headbar2", False):
                self.p.flush(f"abh2_{h}")
        self.flush("attn_b")
        st.close()

    def attn_b(self, src, dst, seq, src_row0=0, dst_row0=0):
        lnsub = ExitStack()
        self.ln_x = self.sb(lnsub, "ln_x", [128, IC * S], BF16)
        self.ln(src, src_row0)
        self.attn_b_lats()
        lnsub.close()
        sub = ExitStack()
        qhalf = self.sb(sub, "ab_qh", [128, 8 * S], BF16)
        khalf = self.sb(sub, "ab_kh", [128, 8 * S], BF16)
        kvlat = self.sb(sub, "ab_kl", [128, LC * S], BF16)
        self.attn_b_ups(qhalf, khalf, kvlat)
        self.ab_te, self.ab_rcd, self.ab_nrm, self.ab_mul = {}, {}, {}, {}
        self.ab_dn = {}
        self.ab_nbg = 0
        self.ab_lnd = 0
        self.attn_b_heads(qhalf, khalf, kvlat)
        sub.close()
        self.proj_scratch("wob", self.t["aout"], IC, self.t["wob"],
                          self.t["wob_bf"], "first" if seq == 0 else "use",
                          dst, src, self.bias["bob"], src_row0, dst_row0)

    def attn_a_heads(self, whb=None, pre=None):
        nc, p, s, t = self.nc, self.p, self.s, self.t
        st = ExitStack()
        if whb is None:
            whb = [self.sb(st, f"aa_w{i}", [128, 3 * D], BF16)
                   for i in range(2)]
        e1 = self.sb(st, "aa_e1", [128, S], F32)
        e2 = self.sb(st, "aa_e2", [128, S], F32)
        qh = self.sb(st, "aa_q", [128, S], BF16)
        kfm = self.sb(st, "aa_kf", [128, S], BF16)
        vfm = self.sb(st, "aa_vf", [128, S], BF16)
        ktm = self.sb(st, "aa_kt", [128, S], BF16)
        vtm = self.sb(st, "aa_vt", [128, S], BF16)
        kvb = self.sb(st, "aa_kv", [128, 128], BF16)
        ksc = self.sb(st, "aa_ks", [128, 1], F32)
        ksb = self.sb(st, "aa_kb", [128, 128], BF16)
        plane = self.sb(st, "aa_pl", [128, S], BF16)
        ob = [self.sb(st, f"aa_o{i}", [128, S], BF16) for i in range(2)]
        PA = self.pt(st, "aa_pa", [128, 2048], F32)
        PB = self.pt(st, "aa_pb", [128, 1024], F32)
        PT = [self.pt(st, f"aa_pt{i}", [128, 1024], BF16) for i in range(2)]
        pe, act, dve = s["pe"], s["act"], s["dve"]
        dg, ds = self.dg, self.ds

        wld = {}
        self.aa_lastln = {}
        head_end = {}
        mult_done = {}
        kv_evac = {}
        vte_done = {}
        ostore = {}

        def load_h(h):
            b = h % 2
            w = [(pe, head_end.get(h - 2, 0))]
            for j, wn in enumerate(("wq", "wk", "wv")):
                wld[h] = self.ld("gpsimd", dg[b], whb[b][:, j * D:(j + 1) * D],
                                 t[wn][h * 128:(h + 1) * 128, :],
                                 w if j == 0 else ())

        load_h(0)
        load_h(1)
        for h in range(NH):
            b = h % 2
            proj_mm = {}
            add_done = {}
            rel_done = {}
            for pj, (wn_off, bn) in enumerate(
                    [(0, "bq"), (1, "bk"), (2, "bv")]):
                for cg in range(NT):
                    for ic in range(IC):
                        w = []
                        if ic == 0:
                            w = [(dg[b], wld[h])]
                            if pj == 0:
                                w += [(dve, mult_done.get(h - 1, 0))]
                            else:
                                w += [(act, rel_done[pj - 1])]
                        p.emit("tensor", lambda e, b=b, cg=cg, ic=ic,
                               wo=wn_off: e.matmul(
                                   PA[:, cg * TT:(cg + 1) * TT],
                                   whb[b][:, wo * D + ic * 128:
                                          wo * D + (ic + 1) * 128],
                                   self.ln_x[:, ic * S + cg * TT:
                                             ic * S + cg * TT + TT],
                                   start=(ic == 0), stop=(ic == IC - 1)),
                               wait=w, inc=pe if ic == IC - 1 else None)
                    proj_mm[(pj, cg)] = pe.n
                if pj < 2:
                    bp = self.bias[bn]
                    bn_ = self.negb[bn]
                    for cg in range(NT):
                        p.emit("scalar", lambda e, cg=cg, bp=bp, h=h: e.activation(
                            e1[:, cg * TT:(cg + 1) * TT],
                            PA[:, cg * TT:(cg + 1) * TT], AF.Relu,
                            bias=bp[:, h:h + 1]),
                            wait=[(pe, proj_mm[(pj, cg)]),
                                  (dve, add_done.get(pj - 1,
                                                     mult_done.get(h - 1, 0)))]
                            if cg == 0 else [(pe, proj_mm[(pj, cg)])])
                        p.emit("scalar", lambda e, cg=cg, bn_=bn_, h=h: e.activation(
                            e2[:, cg * TT:(cg + 1) * TT],
                            PA[:, cg * TT:(cg + 1) * TT], AF.Relu,
                            bias=bn_[:, h:h + 1], scale=-1.0),
                            inc=act if cg == NT - 1 else None)
                    rel_done[pj] = act.n
                    for cg in range(NT):
                        p.emit("scalar", lambda e, cg=cg: e.activation(
                            e2[:, cg * TT:(cg + 1) * TT],
                            e2[:, cg * TT:(cg + 1) * TT], AF.Exp, scale=-1.0),
                            wait=[(act, rel_done[pj])] if cg == 0 else [],
                            inc=act if cg == NT - 1 else None)
                    exp_done = act.n
                    dstq = qh if pj == 0 else kfm
                    for cg in range(NT):
                        p.emit("vector", lambda e, cg=cg, dstq=dstq:
                               e.tensor_add(
                                   dstq[:, cg * TT:(cg + 1) * TT],
                                   e1[:, cg * TT:(cg + 1) * TT],
                                   e2[:, cg * TT:(cg + 1) * TT]),
                               wait=[(act, exp_done)] if cg == 0 else [],
                               inc=dve if cg == NT - 1 else None)
                    add_done[pj] = dve.n
                else:
                    bp = self.bias[bn]
                    for cg in range(NT):
                        p.emit("vector", lambda e, cg=cg, bp=bp, h=h:
                               e.tensor_scalar_add(
                                   vfm[:, cg * TT:(cg + 1) * TT],
                                   PA[:, cg * TT:(cg + 1) * TT],
                                   bp[:, h:h + 1]),
                               wait=[(pe, proj_mm[(pj, cg)])],
                               inc=dve if cg == NT - 1 else None)
                    rel_done[pj] = dve.n
            v_evac = rel_done[2]
            te_cnt = {}
            for src_t, dst_t, dep_sem, dep_val in (
                    (kfm, ktm, dve, add_done[1]), (vfm, vtm, dve, v_evac)):
                for g in range(2):
                    pt = PT[g]
                    for c8 in range(8):
                        c = g * 8 + c8
                        w = []
                        if c8 == 0:
                            w = [(dep_sem, dep_val),
                                 (dve, vte_done.get(h - 1, 0)
                                  if src_t is kfm else te_cnt.get(g, 0))]

                        p.emit("tensor", lambda e, pt=pt, c8=c8, c=c,
                               src_t=src_t: e.transpose(
                                   pt[:, c8 * 128:(c8 + 1) * 128],
                                   src_t[:, c * 128:(c + 1) * 128],
                                   self.ident[:, :]),
                               wait=w, inc=pe if c8 == 7 else None)
                    tp_done = pe.n
                    for c8 in range(8):
                        c = g * 8 + c8
                        p.emit("vector", lambda e, pt=pt, c8=c8, c=c,
                               dst_t=dst_t: e.tensor_copy(
                                   dst_t[:, c * 128:(c + 1) * 128],
                                   pt[:, c8 * 128:(c8 + 1) * 128]),
                               wait=[(pe, tp_done)] if c8 == 0 else [],
                               inc=dve if c8 == 7 else None)
                    te_cnt[g] = dve.n
            vte_done[h] = dve.n
            for c in range(IC):
                p.emit("tensor", lambda e, c=c: e.matmul(
                    PB[:, 0:128], ktm[:, c * 128:(c + 1) * 128],
                    vtm[:, c * 128:(c + 1) * 128],
                    start=(c == 0), stop=(c == IC - 1)),
                    wait=[(dve, vte_done[h]),
                          (act, kv_evac.get(h - 1, 0))] if c == 0 else [])
            for c in range(IC):
                p.emit("tensor", lambda e, c=c: e.matmul(
                    PB[:, 128:129], ktm[:, c * 128:(c + 1) * 128],
                    self.ones[:, 0:1],
                    start=(c == 0), stop=(c == IC - 1)),
                    inc=pe if c == IC - 1 else None)
            kvmm = pe.n
            p.emit("scalar", lambda e: e.activation(
                kvb[:, :], PB[:, 0:128], AF.Copy), wait=[(pe, kvmm)])
            p.emit("scalar", lambda e: e.activation(
                ksc[:, :], PB[:, 128:129], AF.Copy), inc=act)
            kv_evac[h] = act.n
            p.emit("vector", lambda e: e.tensor_scalar(
                ksb[:, :], self.ones[:, :], ksc[:, 0:1], None,
                AluOpType.mult),
                wait=[(act, kv_evac[h])], inc=dve)
            ksb_done = dve.n
            for cg in range(NT):
                p.emit("tensor", lambda e, cg=cg: e.matmul(
                    PA[:, cg * TT:(cg + 1) * TT], kvb[:, :],
                    qh[:, cg * TT:(cg + 1) * TT], start=True, stop=True),
                    wait=[(dve, ksb_done)] if cg == 0 else [],
                    inc=pe if cg == NT - 1 else None)
            qkv_mm = pe.n
            dn_mm, ln_dn = {}, {}
            for cg in range(NT):
                p.emit("tensor", lambda e, cg=cg: e.matmul(
                    PB[:, 512:1024], ksb[:, :],
                    qh[:, cg * TT:(cg + 1) * TT], start=True, stop=True),
                    wait=([(act, ln_dn.get(cg - 1, 0))] if cg > 0 else
                          [(dve, ksb_done), (act, self.aa_lastln.get(0, 0))]),
                    inc=pe)
                dn_mm[cg] = pe.n
                p.emit("scalar", lambda e, cg=cg: e.activation(
                    e2[:, cg * TT:(cg + 1) * TT], PB[:, 512:1024],
                    AF.Ln, bias=1e-8),
                    wait=[(pe, dn_mm[cg]), (dve, add_done[1])], inc=act)
                ln_dn[cg] = act.n
                p.emit("scalar", lambda e, cg=cg: e.activation(
                    plane[:, cg * TT:(cg + 1) * TT],
                    e2[:, cg * TT:(cg + 1) * TT], AF.Exp, scale=-1.0),
                    wait=[(act, ln_dn[cg])], inc=act)
            rc_done = act.n
            self.aa_lastln[0] = ln_dn[NT - 1]
            for cg in range(NT):
                p.emit("vector", lambda e, cg=cg, ob=ob[b]: e.tensor_mul(
                    ob[:, cg * TT:(cg + 1) * TT],
                    PA[:, cg * TT:(cg + 1) * TT],
                    plane[:, cg * TT:(cg + 1) * TT]),
                    wait=[(pe, qkv_mm), (act, rc_done),
                          (ds[b], ostore.get(h - 2, 0))] if cg == 0 else [],
                    inc=dve if cg == NT - 1 else None)
            mult_done[h] = dve.n
            head_end[h] = pe.n
            adst = t["aout"].rearrange("p (q c t) -> c p q t", c=IC, t=TT)[h]
            ostore[h] = self.stq(
                ds[b], adst, ob[b][:, :].rearrange("p (q t) -> p q t", t=TT),
                [(dve, mult_done[h])])
            if h + 2 < NH:
                load_h(h + 2)
        self.flush("attn_a")
        st.close()

    def attn_a(self, src, dst, seq, src_row0=0, dst_row0=0):
        sub = ExitStack()
        self.ln_x = self.sb(sub, "ln_x", [128, IC * S], BF16)
        self.ln(src, src_row0)
        self.attn_a_heads()
        sub.close()
        self.proj_scratch("wo", self.t["aout"], IC, self.t["wo"],
                          self.t["wo_bf"], "first" if seq == 0 else "use",
                          dst, src, self.bias["bo"], src_row0, dst_row0)

    def ffn(self, src, dst, src_row0=0, dst_row0=0):
        sub = ExitStack()
        self.ln_x = self.sb(sub, "ln_x", [128, IC * S], BF16)
        self.ln(src, src_row0)
        self.ffn1()
        sub.close()
        self.proj_scratch("wd", self.t["gated"], HC, self.t["wd"],
                          self.t["wd_bf"],
                          "first" if self.ffn_call == 0 else "use",
                          dst, src, None, src_row0, dst_row0)
        self.ffn_call += 1

    def finish(self):
        self.flush("fin")
        self.st.close()
        return self.nc


def _pack_inputs(inp):
    m = {}
    x = np.asarray(inp["L0_attn_a_x"])
    m["_x_full"] = np.ascontiguousarray(x.transpose(0, 2, 1))
    for k, nm in [("wq", "L0_attn_a_wq"), ("wk", "L0_attn_a_wk"),
                  ("wv", "L0_attn_a_wv"), ("wo", "L0_attn_a_wo"),
                  ("wqd", "L0_attn_b_Wq_down"), ("wkvd", "L0_attn_b_Wkv_down"),
                  ("wqu", "L0_attn_b_Wq_up"), ("wku", "L0_attn_b_Wk_up"),
                  ("wvu", "L0_attn_b_Wv_up"), ("wob", "L0_attn_b_Wo"),
                  ("wg", "L0_ffn_w_gate"), ("wu", "L0_ffn_w_up"),
                  ("wd", "L0_ffn_w_down")]:
        m[k] = pack_w(np.asarray(inp[nm]))
    for k, nm in [("bq", "L0_attn_a_bq"), ("bk", "L0_attn_a_bk"),
                  ("bv", "L0_attn_a_bv"), ("bo", "L0_attn_a_bo"),
                  ("bqd", "L0_attn_b_bq_down"), ("bkvd", "L0_attn_b_bkv_down"),
                  ("bqu", "L0_attn_b_bq_up"), ("bku", "L0_attn_b_bk_up"),
                  ("bvu", "L0_attn_b_bv_up"), ("bob", "L0_attn_b_bo")]:
        m[k] = pack_b(np.asarray(inp[nm]))
    return m


def _run_nc(nc, inp, n_cores=8, trace=False):
    m = _pack_inputs(inp)
    xf = m.pop("_x_full")
    in_maps = []
    for c in range(n_cores):
        d = dict(m)
        d["x"] = np.ascontiguousarray(
            xf[c * NSEQ:(c + 1) * NSEQ].reshape(NSEQ * D, S))
        in_maps.append(d)
    res = run_bass_kernel_spmd(nc, in_maps, core_ids=list(range(n_cores)),
                               trace=trace)
    y = np.concatenate([r["y"].reshape(NSEQ, D, S) for r in res.results], 0)
    return np.ascontiguousarray(y.transpose(0, 2, 1)), res


def _build_ffn_only():
    mk = MK()
    mk.setup()
    for seq in range(NSEQ):
        mk.ffn_call = seq
        mk.ffn(mk.t["x"], mk.t["y"], src_row0=seq * D, dst_row0=seq * D)
    return mk.finish()


def _build_full():
    mk = MK()
    t = mk.t
    mk.setup()
    mk.rope_tables()
    for seq in range(NSEQ):
        row = seq * D
        mk.attn_a(t["x"], t["xs0"], seq, src_row0=row, dst_row0=0)
        mk.ffn(t["xs0"], t["xs1"])
        mk.attn_b(t["xs1"], t["xs0"], seq)
        mk.ffn(t["xs0"], t["y"], dst_row0=row)
    return mk.finish()


def kernel(**inp):
    nc = _build_full()
    y, _ = _run_nc(nc, inp, n_cores=8)
    return y
